# version 03
# baseline (speedup 1.0000x reference)
import numpy as np
import ml_dtypes
import concourse.bass as bass
import concourse.mybir as mybir
from concourse.ap import AP
from concourse.bass_utils import run_bass_kernel_spmd
from contextlib import ExitStack
from types import SimpleNamespace

F32 = mybir.dt.float32
BF16 = mybir.dt.bfloat16
U8 = mybir.dt.uint8
AF = mybir.ActivationFunctionType
ALU = mybir.AluOpType
AXX = mybir.AxisListType.X

NB = 8
D = 1024
DCH = 8
SLICE = 512
L = 32
W = 80
NSTEP = W + L
NCHK = SLICE // L
XBW = SLICE + W
T1OFF = [0, W, W + 128, W + 256, W + 384]
T1N = [W, 128, 128, 128, 128]
NT1 = 5
EPS = 1e-5

KINDS = ["sync", "gpsimd", "tensor", "vector", "scalar"]
WSEQ = ["B_T", "A_T", "C_T", "wf_T", "wi_T", "wc_T", "uf_T", "ui_T", "uc_T", "wo_T", "uo_T"]
WOFF = {n: 8 * i for i, n in enumerate(WSEQ)}
WCASTER = {n: "scalar" for n in
           ["B_T", "A_T", "C_T", "wf_T", "wi_T", "wc_T", "uf_T", "ui_T", "uc_T", "wo_T", "uo_T"]}

TAPS = set()
PHASES = ["p0", "p1", "p2", "p3", "p4a", "p4b"]


class _MockInstr:
    def then_inc(self, *a, **k):
        return self


class _MockEng:
    def __getattr__(self, name):
        def f(*a, **k):
            return _MockInstr()
        return f

    def wait_ge(self, *a, **k):
        pass


class Ctx:
    def __init__(self):
        self.marks = {}
        self.counts = {}
        self.counts_real = {}
        self.sems = {}
        self.dry = True
        self.kind = None
        self.eng = None

    def setup(self, kind, eng, dry):
        self.kind = kind
        self.eng = eng
        self.dry = dry

    def inc(self, instr, semname, n=1, mark=None):
        if self.dry:
            self.counts[semname] = self.counts.get(semname, 0) + n
            if mark is not None:
                assert mark not in self.marks, f"dup mark {mark}"
                self.marks[mark] = (semname, self.counts[semname])
        else:
            self.counts_real[semname] = self.counts_real.get(semname, 0) + n
            if mark is not None:
                assert self.marks.get(mark) == (semname, self.counts_real[semname]), (
                    f"mark drift {mark}: dry {self.marks.get(mark)} real "
                    f"({semname}, {self.counts_real[semname]})")
        instr.then_inc(self.sems[semname], n)

    def wait(self, mark):
        if self.dry:
            return
        semname, v = self.marks[mark]
        self.eng.wait_ge(self.sems[semname], v)

    def wait_if(self, mark):
        if self.dry:
            return
        if mark in self.marks:
            semname, v = self.marks[mark]
            self.eng.wait_ge(self.sems[semname], v)


def _w_dma_mark(i):
    return f"q.w.{i}"


def _w_cast_mark(i):
    return f"c.w.{i}"


def emit_weight_dmas(C, A, name):
    if C.kind != "sync":
        return
    base = WOFF[name]
    for ch in range(8):
        i = base + ch
        if i >= 2:
            C.wait(_w_cast_mark(i - 2))
        ins = C.eng.dma_start(out=A.wstage[i % 2][:, :], in_=A.dram[name][ch * 128:(ch + 1) * 128, :])
        C.inc(ins, "qS", 16, mark=_w_dma_mark(i))


def emit_weight_casts(C, A, name, dest):
    if C.kind != WCASTER[name]:
        return
    base = WOFF[name]
    for ch in range(8):
        i = base + ch
        C.wait(_w_dma_mark(i))
        ins = C.eng.activation(dest[ch][:, :], A.wstage[i % 2][:, :], AF.Identity)
        C.inc(ins, "sS", 1, mark=_w_cast_mark(i))


def emit_row_bcast(C, A, tag, src_bf, col_view, bc_dests, war_mark=None):
    if C.kind == "tensor":
        if war_mark is not None:
            C.wait(war_mark)
        C.wait("v.g1fm")
        ins = C.eng.transpose(A.rowT_ps[:, :], col_view, A.ident_b[:, :])
        C.inc(ins, "sT", 1, mark=f"t.rowT.{tag}")
        C.wait(f"q.rowflat.{tag}")
        for h in range(2):
            ins = C.eng.matmul(A.bc_ps[:, h * 512:(h + 1) * 512], A.ones_b[0:1, :],
                               A.rowflat[0:1, h * 512:(h + 1) * 512], start=True, stop=True)
        C.inc(ins, "sT", 1, mark=f"t.bc.{tag}")
    elif C.kind == "vector":
        C.wait(f"t.rowT.{tag}")
        ins = C.eng.tensor_copy(A.rowstage[:, :], A.rowT_ps[:, :])
        C.inc(ins, "sV", 1, mark=f"v.rowev.{tag}")
        C.wait(f"t.bc.{tag}")
        ins = C.eng.tensor_copy(bc_dests[:, :], A.bc_ps[:, :])
        C.inc(ins, "sV", 1, mark=f"v.bcev.{tag}")
    elif C.kind == "sync":
        C.wait(f"v.rowev.{tag}")
        ins = C.eng.dma_start(out=A.rowflat[0:1, :], in_=A.rowstage[:, :])
        C.inc(ins, "qS", 16, mark=f"q.rowflat.{tag}")


def xb_view(A, e, t):
    return AP(A.xb_b[e][:, :].tensor, t, [[NB * XBW, 128], [XBW, NB], [L, NCHK]])


def hv_view(A, e, off):
    return AP(A.H_valid[e][:, :].tensor, off, [[NB * SLICE, 128], [SLICE, NB], [L, NCHK]])


def emit_p0(C, A):
    if C.kind == "sync":
        for nm, dst in [("ident", A.ident_b), ("vecs", A.vecs_fm), ("c_in", A.c_fm),
                        ("pmask", A.pmask_b)]:
            ins = C.eng.dma_start(out=dst[:, :], in_=A.dram[nm])
            C.inc(ins, "qS", 16, mark=f"q.{nm}")
        for d in range(8):
            if d >= 2:
                C.wait(f"c.wmod.{d - 2}")
            ins = C.eng.dma_start(out=A.wmodstage[d % 2][:, :], in_=A.dram["wmod"][d * 128:(d + 1) * 128, :])
            C.inc(ins, "qS", 16, mark=f"q.wmod.{d}")
    elif C.kind == "scalar":
        C.wait("q.c_in")
        ins = C.eng.activation(A.silu_c[:, :], A.c_fm[:, :], AF.Silu)
        C.inc(ins, "sS", 1, mark="s.silu0")
        C.wait("s.silu0")
        ins = C.eng.activation(A.silu_cb[:, :], A.silu_c[:, :], AF.Identity)
        C.inc(ins, "sS", 1, mark="s.silu")
    elif C.kind == "tensor":
        C.wait("s.silu")
        for d in range(8):
            C.wait(f"c.wmod.{d}")
            if d >= 2:
                C.wait(f"v.modev.{d - 2}")
            for mch in range(48):
                ins = C.eng.matmul(A.mod_ps[d % 2][:, mch * 8:(mch + 1) * 8],
                                   A.wmod_b[d % 2][:, mch * 128:(mch + 1) * 128],
                                   A.silu_cb[:, d * 8:(d + 1) * 8],
                                   start=True, stop=True)
            C.inc(ins, "sT", 1, mark=f"t.mod.{d}")
    if C.kind == "scalar":
        for d in range(8):
            C.wait(f"q.wmod.{d}")
            if d >= 2:
                C.wait(f"t.mod.{d - 2}")
            ins = C.eng.activation(A.wmod_b[d % 2][:, :], A.wmodstage[d % 2][:, :], AF.Identity)
            C.inc(ins, "sS", 1, mark=f"c.wmod.{d}")
    if C.kind == "gpsimd":
        if "mod" in TAPS:
            C.wait("v.g1fm")
            ins = C.eng.dma_start(out=A.dram["tap_mod"], in_=A.mod_fm[:, :])
            C.inc(ins, "qG", 16)
    elif C.kind == "vector":
        ins = C.eng.memset(A.ones_b[0:1, :], 1.0)
        C.inc(ins, "sV", 1)
        ins = C.eng.memset(A.mod_fm[:, :], 0.0)
        C.inc(ins, "sV", 1, mark="v.mod0")
        for d in range(8):
            C.wait(f"t.mod.{d}")
            C.wait("v.mod0" if d == 0 else f"v.modev.{d - 1}")
            ins = C.eng.tensor_tensor(A.mod_fm[:, :], A.mod_fm[:, :], A.mod_ps[d % 2][:, :], ALU.add)
            C.inc(ins, "sV", 1, mark=f"v.modev.{d}")
        C.wait("v.modev.7")
        C.inc(_MockInstr(), "sV", 0, mark="v.mod")
        e = C.eng
        C.inc(e.tensor_scalar_add(A.onepca[:, :], A.mod_fm[:, 64:128], 1.0), "sV", 1)
        C.inc(e.tensor_scalar_add(A.onepcb[:, :], A.mod_fm[:, 256:320], 1.0), "sV", 1)
        for ci, (src, dst) in enumerate([(A.mod_fm[:, 128:192], A.ga_cl), (A.mod_fm[:, 320:384], A.gb_cl)]):
            C.inc(e.tensor_scalar_max(A.cl_m1[:, :], src, 0.25), "sV", 1)
            C.inc(e.tensor_scalar_min(A.cl_m2[:, :], src, -0.25), "sV", 1)
            C.inc(e.tensor_single_scalar(A.cl_mask[:, :], src, 0.0, ALU.is_ge), "sV", 1, mark=f"v.clin.{ci}")
            C.wait(f"v.clin.{ci}")
            C.inc(e.select(dst[:, :], A.cl_mask[:, :], A.cl_m1[:, :], A.cl_m2[:, :]), "sV", 1, mark=f"v.clsel.{ci}")
        C.wait("q.vecs")
        C.wait("v.clsel.1")
        for fc in range(8):
            gcol = A.vecs_fm[:, fc:fc + 1].broadcast_to((128, 8))
            bcol = A.vecs_fm[:, 8 + fc:8 + fc + 1].broadcast_to((128, 8))
            C.inc(e.tensor_mul(A.G1_fm[:, fc * 8:(fc + 1) * 8], A.ga_cl[:, fc * 8:(fc + 1) * 8], gcol), "sV", 1)
            ins = e.tensor_mul(A.B1_fm[:, fc * 8:(fc + 1) * 8], A.ga_cl[:, fc * 8:(fc + 1) * 8], bcol)
            C.inc(ins, "sV", 1, mark=f"v.g1prod.{fc}" if fc == 7 else None)
        C.wait("v.g1prod.7")
        C.inc(_MockInstr(), "sV", 0, mark="v.g1f")
    if C.kind == "scalar":
        e2 = C.eng
        C.wait("v.g1f")
        C.inc(e2.activation(A.G1bfm[:, :], A.G1_fm[:, :], AF.Identity), "sS", 1)
        C.inc(e2.activation(A.B1bfm[:, :], A.B1_fm[:, :], AF.Identity), "sS", 1)
        C.inc(e2.activation(A.GBbfm[:, :], A.gb_cl[:, :], AF.Identity), "sS", 1)
        C.wait("q.vecs")
        C.inc(e2.activation(A.BObfm[:, :], A.vecs_fm[:, 40:48], AF.Identity), "sS", 1, mark="v.g1fm")


def emit_p1(C, A):
    e = C.eng
    for g in range(NB):
        if C.kind == "sync":
            if g == 0:
                emit_weight_dmas(C, A, "B_T")
            for i in range(NT1):
                gt = g * NT1 + i
                if gt >= 2:
                    C.wait(f"s.xhat.{(gt - 2) // NT1}.{(gt - 2) % NT1}")
                if g >= 1:
                    C.wait(f"t.xb.{g - 1}.3.3")
                n = T1N[i]
                src = A.dram["x_pref"][g, :, :] if i == 0 else A.dram["x_main"][g, (i - 1) * 128:i * 128, :]
                ins = e.dma_start(out=A.xstage[gt % 2][0:n, :], in_=src)
                C.inc(ins, "qS", 16, mark=f"q.x1.{g}.{i}")
        if g == 0:
            emit_weight_casts(C, A, "B_T", A.B_Tb)
        for i in range(NT1):
            n = T1N[i]
            st = A.xstage[(g * NT1 + i) % 2]
            xh = A.xh_bf[(g * NT1 + i) % 2]
            ic = i % 2
            if C.kind == "vector":
                C.wait(f"q.x1.{g}.{i}")
                C.inc(e.reduce_sum(A.s_sum[0:n, ic:ic + 1], st[0:n, :], axis=AXX), "sV", 1,
                      mark=f"v.rsum.{g}.{i}")
                C.wait(f"v.rsum.{g}.{i}")
                C.inc(e.tensor_scalar_mul(A.s_mneg[0:n, ic:ic + 1], A.s_sum[0:n, ic:ic + 1], -1.0 / D),
                      "sV", 1, mark=f"v.mu.{g}.{i}")
                C.wait(f"s.sq.{g}.{i}")
                C.inc(e.tensor_scalar(A.s_var[0:n, ic:ic + 1], A.s_ssq[0:n, ic:ic + 1],
                                      scalar1=1.0 / D, scalar2=EPS, op0=ALU.mult, op1=ALU.add),
                      "sV", 1, mark=f"v.var.{g}.{i}")
                C.wait(f"s.sqrt.{g}.{i}")
                C.inc(e.reciprocal(A.s_rstd[0:n, ic:ic + 1], A.s_std[0:n, ic:ic + 1]), "sV", 1,
                      mark=f"v.rcp.{g}.{i}")
                C.wait(f"v.rcp.{g}.{i}")
                C.inc(e.tensor_mul(A.s_bias[0:n, ic:ic + 1], A.s_mneg[0:n, ic:ic + 1], A.s_rstd[0:n, ic:ic + 1]),
                      "sV", 1, mark=f"v.rstd.{g}.{i}")
            elif C.kind == "scalar":
                C.wait(f"v.mu.{g}.{i}")
                if g >= 1:
                    C.wait(f"t.xb.{g - 1}.3.3")
                gt = g * NT1 + i
                if gt >= 2:
                    pg, pi = (gt - 2) // NT1, (gt - 2) % NT1
                    C.wait(f"t.tr.{pg}.{pi}.1")
                C.inc(e.activation(xh[0:n, :], st[0:n, :], AF.Square,
                                   bias=A.s_mneg[0:n, ic:ic + 1], accum_out=A.s_ssq[0:n, ic:ic + 1]),
                      "sS", 1, mark=f"s.sq.{g}.{i}")
                C.wait(f"v.var.{g}.{i}")
                C.inc(e.sqrt(A.s_std[0:n, ic:ic + 1], A.s_var[0:n, ic:ic + 1]), "sS", 1, mark=f"s.sqrt.{g}.{i}")
                C.wait(f"v.rstd.{g}.{i}")
                C.inc(e.activation(xh[0:n, :], st[0:n, :], AF.Identity,
                                   bias=A.s_bias[0:n, ic:ic + 1], scale=A.s_rstd[0:n, ic:ic + 1]),
                      "sS", 1, mark=f"s.xhat.{g}.{i}")
            if C.kind == "tensor":
                C.wait(f"s.xhat.{g}.{i}")
                C.wait("v.g1fm")
                for half in range(2):
                    gt = g * NT1 + i
                    if gt >= 1:
                        pg, pi = (gt - 1) // NT1, (gt - 1) % NT1
                        C.wait(f"s.h1ev.{pg}.{pi}.{half}")
                    for fq in range(4):
                        fc = half * 4 + fq
                        ins = e.transpose(A.tr_ps[half][:, fq * 128:fq * 128 + n],
                                          xh[0:n, fc * 128:(fc + 1) * 128], A.ident_b[0:n, 0:n])
                    C.inc(ins, "sT", 1, mark=f"t.tr.{g}.{i}.{half}")
            if C.kind == "scalar":
                C.wait("v.g1f")
                for half in range(2):
                    C.wait(f"t.tr.{g}.{i}.{half}")
                    if g >= 1 and i == 0 and half == 0:
                        C.wait(f"v.xbev.{g - 1}.3.3")
                        C.wait(f"v.xbm.{g - 1}")
                    ins = e.activation(A.h1stage[:, :], A.tr_ps[half][:, :], AF.Identity)
                    C.inc(ins, "sS", 1, mark=f"s.h1st.{g}.{i}.{half}")
                    C.wait(f"s.h1st.{g}.{i}.{half}")
                    for fq in range(4):
                        fc = half * 4 + fq
                        sac = A.mod_fm[:, fc * 8 + g:fc * 8 + g + 1]
                        ins = e.activation(A.h1_fm[fc][:, T1OFF[i]:T1OFF[i] + n],
                                           A.h1stage[:, fq * 128:fq * 128 + n], AF.Identity,
                                           scale=A.onepca[:, fc * 8 + g:fc * 8 + g + 1], bias=sac)
                        C.inc(ins, "sS", 1, mark=f"s.h1ev.{g}.{i}.{half}" if fq == 3 else None)
        if C.kind == "scalar":
            ins = e.activation(A.h1stage[:, :], A.h1stage[:, :], AF.Identity)
            C.inc(ins, "sS", 1)
            ins = e.activation(A.h1stage[:, :], A.h1stage[:, :], AF.Identity)
            C.inc(ins, "sS", 1, mark=f"s.h1.{g}")
        if C.kind == "sync":
            C.inc(_MockInstr(), "qS", 0, mark=f"q.h1mm.{g}")
        CR = [(0, 512), (512, XBW)]
        if C.kind == "vector":
            pass
        for wv in range(4):
            eg, cr = wv % 2, wv // 2
            c0, c1 = CR[cr]
            for el in range(4):
                ee = eg * 4 + el
                if C.kind == "tensor":
                    if wv == 0 and el == 0:
                        C.wait(f"s.h1.{g}")
                        if g == 0:
                            C.wait(_w_cast_mark(WOFF["B_T"] + 7))
                    if g > 0 or wv > 0:
                        pw, pgg = wv - 1, g
                        if pw < 0:
                            pw, pgg = 3, g - 1
                        C.wait(f"v.xbev.{pgg}.{pw}.{el}")
                    for d in range(8):
                        ins = e.matmul(A.xb_ps[el][:, 0:c1 - c0],
                                       A.B_Tb[d][:, ee * 128:(ee + 1) * 128],
                                       A.h1_fm[d][:, c0:c1], start=(d == 0), stop=(d == 7))
                    C.inc(ins, "sT", 1, mark=f"t.xb.{g}.{wv}.{el}")
                elif C.kind == "vector":
                    C.wait(f"t.xb.{g}.{wv}.{el}")
                    ins = e.tensor_copy(A.xb_b[ee][:, g * XBW + c0:g * XBW + c1], A.xb_ps[el][:, 0:c1 - c0])
                    C.inc(ins, "sV", 1, mark=f"v.xbev.{g}.{wv}.{el}")
        if C.kind == "vector":
            C.wait("q.pmask")
            C.wait(f"v.xbev.{g}.1.3")
            for ee in range(8):
                ins = e.tensor_mul(A.xb_b[ee][:, g * XBW:g * XBW + W],
                                   A.xb_b[ee][:, g * XBW:g * XBW + W], A.pmask_b[:, :])
                C.inc(ins, "sV", 1, mark=f"v.xbm.{g}" if ee == 7 else None)
    if C.kind == "gpsimd" and "xb" in TAPS:
        C.wait(f"v.xbm.{NB - 1}")
        for ee in range(8):
            ins = e.dma_start(out=A.dram["tap_xb"][ee], in_=A.xb_b[ee][:, :])
            C.inc(ins, "qG", 16)
        for _ in range(64):
            ins = e.tensor_copy(A.xstage[0][:, :], A.xstage[1][:, :])
            C.inc(ins, "qG", 1)
        for ee in range(8):
            ins = e.dma_start(out=A.dram["tap_xb2"][ee], in_=A.xb_b[ee][:, :])
            C.inc(ins, "qG", 16)
    if C.kind == "gpsimd" and "h1" in TAPS:
        C.wait(f"v.xbm.{NB - 1}")
        for fc in range(8):
            ins = e.dma_start(out=A.dram["tap_h1"][fc], in_=A.h1_fm[fc][:, :])
            C.inc(ins, "qG", 16)


def emit_p2(C, A):
    e = C.eng
    if C.kind == "sync":
        emit_weight_dmas(C, A, "A_T")
        emit_weight_dmas(C, A, "C_T")
    emit_weight_casts(C, A, "A_T", A.A_Tb)
    if C.kind == "vector":
        C.wait(f"v.xbm.{NB - 1}")
        for d in range(8):
            ins = e.memset(A.h_par[0][d][:, :], 0.0)
            C.inc(ins, "sV", 1, mark="v.h0" if d == 7 else None)
    emit_weight_casts(C, A, "C_T", A.C_Tb)
    for t in range(NSTEP):
        if C.kind == "tensor":
            if t == 0:
                C.wait("v.h0")
            else:
                C.wait(f"s.rec.{t - 1}.7")
            for ee in range(8):
                for d in range(8):
                    if t == 0:
                        rhs = A.h_par[0][d][:, :]
                    elif t <= W:
                        rhs = A.h_par[t % 2][d][:, :]
                    else:
                        rhs = hv_view(A, d, t - 1 - W)
                    ins = e.matmul(A.pm[ee][:, :], A.A_Tb[d][:, ee * 128:(ee + 1) * 128],
                                   rhs, start=(d == 0), stop=(d == 7))
                C.inc(ins, "sT", 1, mark=f"t.rec.{t}.{ee}")
        elif C.kind == "vector":
            if t >= 2:
                C.wait(f"s.rec.{t - 2}.7")
            for ee in range(8):
                C.wait(f"t.rec.{t}.{ee}")
                ins = e.tensor_tensor(A.rec_tmp[t % 2][ee][:, :], A.pm[ee][:, :], xb_view(A, ee, t), ALU.add)
                C.inc(ins, "sV", 1, mark=f"v.rec.{t}.{ee}")
        elif C.kind == "scalar":
            for ee in range(8):
                C.wait(f"v.rec.{t}.{ee}")
                if t < W:
                    dst = A.h_par[(t + 1) % 2][ee][:, :]
                else:
                    dst = hv_view(A, ee, t - W)
                ins = e.activation(dst, A.rec_tmp[t % 2][ee][:, :], AF.Tanh)
                C.inc(ins, "sS", 1, mark=f"s.rec.{t}.{ee}")
    if C.kind == "gpsimd" and "H" in TAPS:
        C.wait(f"s.rec.{NSTEP - 1}.7")
        for ee in range(8):
            ins = e.dma_start(out=A.dram["tap_H"][ee], in_=A.H_valid[ee][:, :])
            C.inc(ins, "qG", 16)


def emit_p3(C, A):
    e = C.eng
    if C.kind == "tensor":
        C.wait(f"s.rec.{NSTEP - 1}.7")
        C.wait(_w_cast_mark(WOFF["C_T"] + 7))
    for g in range(NB):
        emit_row_bcast(C, A, f"G1.{g}", A.G1bfm,
                       AP(A.G1bfm[:, :].tensor, g, [[64, 128], [8, 8]]), A.G1bc,
                       war_mark=f"v.bcev.B1.{g - 1}" if g > 0 else None)
        emit_row_bcast(C, A, f"B1.{g}", A.B1bfm,
                       AP(A.B1bfm[:, :].tensor, g, [[64, 128], [8, 8]]), A.B1bc,
                       war_mark=f"v.bcev.G1.{g}")
        for k in range(4):
            kc = k % 2
            gk = g * 4 + k
            if C.kind == "sync":
                if gk >= 2:
                    C.wait(f"v.out1.{g if k >= 2 else g - 1}.{(k - 2) % 4}")
                    C.wait(f"p.u.{g if k >= 2 else g - 1}.{(k - 2) % 4}")
                ins = e.dma_start(out=A.x3stage[kc][:, :], in_=A.dram["x_main"][g, k * 128:(k + 1) * 128, :])
                C.inc(ins, "qS", 16, mark=f"q.x3.{g}.{k}")
            elif C.kind == "tensor":
                if gk >= 2:
                    C.wait(f"v.out1.{g if k >= 2 else g - 1}.{(k - 2) % 4}")
                for h in range(2):
                    for ee in range(8):
                        ins = e.matmul(A.y_ps2[kc][h][:, :],
                                       A.H_valid[ee][:, g * SLICE + k * 128:g * SLICE + (k + 1) * 128],
                                       A.C_Tb[ee][:, h * 512:(h + 1) * 512],
                                       start=(ee == 0), stop=(ee == 7))
                C.inc(ins, "sT", 1, mark=f"t.y.{g}.{k}")
            if C.kind == "vector":
                C.wait(f"q.x3.{g}.{k}")
                C.wait(f"t.y.{g}.{k}")
                if gk >= 2:
                    C.wait(f"s.xhat3a.{g if k >= 2 else g - 1}.{(k - 2) % 4}")
                for h in range(2):
                    ins = e.tensor_tensor(A.out1[kc][:, h * 512:(h + 1) * 512], A.y_ps2[kc][h][:, :],
                                          A.x3stage[kc][:, h * 512:(h + 1) * 512], ALU.add)
                    C.inc(ins, "sV", 1, mark=f"v.out1.{g}.{k}" if h == 1 else None)
            elif C.kind == "gpsimd":
                C.wait(f"q.x3.{g}.{k}")
                C.wait(f"v.bcev.B1.{g}")
                if gk >= 2:
                    C.wait(f"v.x1.{g if k >= 2 else g - 1}.{(k - 2) % 4}")
                ins = e.tensor_tensor(A.u_t[kc][:, :], A.x3stage[kc][:, :], A.B1bc[:, :], ALU.add)
                C.inc(ins, "qG", 1, mark=f"p.u.{g}.{k}")
            if C.kind == "vector":
                C.wait(f"v.out1.{g}.{k}")
                C.inc(e.reduce_sum(A.s3_sum[:, kc:kc + 1], A.out1[kc][:, :], axis=AXX), "sV", 1,
                      mark=f"v.rs3.{g}.{k}")
                C.wait(f"v.rs3.{g}.{k}")
                C.inc(e.tensor_scalar_mul(A.s3_mneg[:, kc:kc + 1], A.s3_sum[:, kc:kc + 1], -1.0 / D),
                      "sV", 1, mark=f"v.mu3.{g}.{k}")
                C.wait(f"s.sq3.{g}.{k}")
                C.inc(e.tensor_scalar(A.s3_var[:, kc:kc + 1], A.s3_ssq[:, kc:kc + 1],
                                      scalar1=1.0 / D, scalar2=EPS, op0=ALU.mult, op1=ALU.add),
                      "sV", 1, mark=f"v.var3.{g}.{k}")
                C.wait(f"s.sqrt3.{g}.{k}")
                C.inc(e.reciprocal(A.s3_rstd[:, kc:kc + 1], A.s3_std[:, kc:kc + 1]), "sV", 1,
                      mark=f"v.rcp3.{g}.{k}")
                C.wait(f"v.rcp3.{g}.{k}")
                C.inc(e.tensor_mul(A.s3_bias[:, kc:kc + 1], A.s3_mneg[:, kc:kc + 1], A.s3_rstd[:, kc:kc + 1]),
                      "sV", 1, mark=f"v.rstd3.{g}.{k}")
            elif C.kind == "scalar":
                C.wait(f"v.mu3.{g}.{k}")
                if gk >= 2:
                    C.wait(f"v.tbf.{g if k >= 2 else g - 1}.{(k - 2) % 4}")
                C.inc(e.activation(A.xh3a[kc][:, :], A.out1[kc][:, :], AF.Square,
                                   bias=A.s3_mneg[:, kc:kc + 1], accum_out=A.s3_ssq[:, kc:kc + 1]),
                      "sS", 1, mark=f"s.sq3.{g}.{k}")
                C.wait(f"v.var3.{g}.{k}")
                C.inc(e.sqrt(A.s3_std[:, kc:kc + 1], A.s3_var[:, kc:kc + 1]), "sS", 1, mark=f"s.sqrt3.{g}.{k}")
                C.wait(f"v.rstd3.{g}.{k}")
                C.inc(e.activation(A.xh3a[kc][:, :], A.out1[kc][:, :], AF.Identity,
                                   bias=A.s3_bias[:, kc:kc + 1], scale=A.s3_rstd[:, kc:kc + 1]),
                      "sS", 1, mark=f"s.xhat3a.{g}.{k}")
            if C.kind == "vector":
                C.wait(f"s.xhat3a.{g}.{k}")
                C.wait(f"v.bcev.G1.{g}")
                ins = e.tensor_mul(A.tbf[kc][:, :], A.xh3a[kc][:, :], A.G1bc[:, :])
                C.inc(ins, "sV", 1, mark=f"v.tbf.{g}.{k}")
                C.wait(f"p.u.{g}.{k}")
                C.wait(f"v.tbf.{g}.{k}")
                if gk >= 2:
                    C.wait(f"g.x1sp.{g if k >= 2 else g - 1}.{(k - 2) % 4}")
                ins = e.tensor_tensor(A.x1t[kc][:, :], A.u_t[kc][:, :], A.tbf[kc][:, :], ALU.add)
                C.inc(ins, "sV", 1, mark=f"v.x1.{g}.{k}")
                C.wait(f"v.x1.{g}.{k}")
                C.inc(e.reduce_sum(A.s3_sum2[:, kc:kc + 1], A.x1t[kc][:, :], axis=AXX), "sV", 1,
                      mark=f"v.rs3b.{g}.{k}")
                C.wait(f"v.rs3b.{g}.{k}")
                C.inc(e.tensor_scalar_mul(A.s3_mneg2[:, kc:kc + 1], A.s3_sum2[:, kc:kc + 1], -1.0 / D),
                      "sV", 1, mark=f"v.mu3b.{g}.{k}")
                C.wait(f"s.sq3b.{g}.{k}")
                C.inc(e.tensor_scalar(A.s3_var2[:, kc:kc + 1], A.s3_ssq2[:, kc:kc + 1],
                                      scalar1=1.0 / D, scalar2=EPS, op0=ALU.mult, op1=ALU.add),
                      "sV", 1, mark=f"v.var3b.{g}.{k}")
                C.wait(f"s.sqrt3b.{g}.{k}")
                C.inc(e.reciprocal(A.s3_rstd2[:, kc:kc + 1], A.s3_std2[:, kc:kc + 1]), "sV", 1,
                      mark=f"v.rcp3b.{g}.{k}")
                C.wait(f"v.rcp3b.{g}.{k}")
                C.inc(e.tensor_mul(A.s3_bias2[:, kc:kc + 1], A.s3_mneg2[:, kc:kc + 1], A.s3_rstd2[:, kc:kc + 1]),
                      "sV", 1, mark=f"v.rstd3b.{g}.{k}")
            elif C.kind == "gpsimd":
                C.wait(f"v.x1.{g}.{k}")
                ins = e.dma_start(out=A.x1_hbm[g, k * 128:(k + 1) * 128, :], in_=A.x1t[kc][:, :])
                C.inc(ins, "qG", 16, mark=f"g.x1sp.{g}.{k}")
            if C.kind == "scalar":
                C.wait(f"v.mu3b.{g}.{k}")
                if gk >= 2:
                    C.wait(f"t.tr3.{g if k >= 2 else g - 1}.{(k - 2) % 4}.1")
                C.inc(e.activation(A.xh3b[kc][:, :], A.x1t[kc][:, :], AF.Square,
                                   bias=A.s3_mneg2[:, kc:kc + 1], accum_out=A.s3_ssq2[:, kc:kc + 1]),
                      "sS", 1, mark=f"s.sq3b.{g}.{k}")
                C.wait(f"v.var3b.{g}.{k}")
                C.inc(e.sqrt(A.s3_std2[:, kc:kc + 1], A.s3_var2[:, kc:kc + 1]), "sS", 1, mark=f"s.sqrt3b.{g}.{k}")
                C.wait(f"v.rstd3b.{g}.{k}")
                C.inc(e.activation(A.xh3b[kc][:, :], A.x1t[kc][:, :], AF.Identity,
                                   bias=A.s3_bias2[:, kc:kc + 1], scale=A.s3_rstd2[:, kc:kc + 1]),
                      "sS", 1, mark=f"s.xhat3b.{g}.{k}")
            if C.kind == "tensor":
                C.wait(f"s.xhat3b.{g}.{k}")
                for half in range(2):
                    th = gk * 2 + half
                    if th >= 1:
                        pg, pk, ph = (th - 1) // 8, ((th - 1) % 8) // 2, (th - 1) % 2
                        C.wait(f"v.h2ev.{pg}.{pk}.{ph}")
                    for fq in range(4):
                        fc = half * 4 + fq
                        ins = e.transpose(A.tr_ps2[:, fq * 128:(fq + 1) * 128],
                                          A.xh3b[kc][:, fc * 128:(fc + 1) * 128], A.ident_b[:, :])
                    C.inc(ins, "sT", 1, mark=f"t.tr3.{g}.{k}.{half}")
            if C.kind == "vector":
                for half in range(2):
                    C.wait(f"t.tr3.{g}.{k}.{half}")
                    if g >= 1 and k == 0 and half == 0:
                        C.wait(f"g.h2sp.{g - 1}")
                    for fq in range(4):
                        fc = half * 4 + fq
                        sbc = A.mod_fm[:, (24 + fc) * 8 + g:(24 + fc) * 8 + g + 1]
                        ins = e.tensor_scalar(A.h2cur[fc][:, k * 128:(k + 1) * 128],
                                              A.tr_ps2[:, fq * 128:(fq + 1) * 128],
                                              scalar1=A.onepcb[:, fc * 8 + g:fc * 8 + g + 1],
                                              scalar2=sbc, op0=ALU.mult, op1=ALU.add)
                        C.inc(ins, "sV", 1, mark=f"v.h2ev.{g}.{k}.{half}" if fq == 3 else None)
        if C.kind == "gpsimd":
            C.wait(f"v.h2ev.{g}.3.1")
            for fc in range(8):
                ins = e.dma_start(out=A.h2_hbm[g, fc], in_=A.h2cur[fc][:, :])
                C.inc(ins, "qG", 16, mark=f"g.h2sp.{g}" if fc == 7 else None)


def emit_p4a(C, A):
    e = C.eng
    for nm in ["wf_T", "wi_T", "wc_T", "uf_T", "ui_T", "uc_T"]:
        emit_weight_dmas(C, A, nm)
    for nm, dst in [("wf_T", A.wgb[0]), ("wi_T", A.wgb[1]), ("wc_T", A.wgb[2]),
                    ("uf_T", A.ugb[0]), ("ui_T", A.ugb[1]), ("uc_T", A.ugb[2])]:
        emit_weight_casts(C, A, nm, dst)
    if C.kind == "vector":
        ins = e.memset(A.ones512[:, :], 1.0)
        C.inc(ins, "sV", 1, mark="v.ones512")
        C.wait("q.vecs")
        C.wait("v.ones512")
        for mch in range(8):
            ins = e.tensor_scalar_mul(A.c_f32[0][mch][:, :], A.ones512[:, :],
                                      A.vecs_fm[:, 48 + mch:48 + mch + 1])
            C.inc(ins, "sV", 1, mark=f"v.cadd.-1.{mch}")
    if C.kind == "scalar":
        for mch in range(8):
            C.wait(f"v.cadd.-1.{mch}")
            ins = e.activation(A.c_bf[0][mch][:, :], A.c_f32[0][mch][:, :], AF.Identity)
            C.inc(ins, "sS", 1, mark=f"s.cup.-1.{mch}")
    for t in range(NB):
        if C.kind == "sync":
            if t >= 2:
                C.wait(f"t.wdone.{t - 2}")
            C.wait(f"g.h2sp.{t}")
            for fc in range(8):
                ins = e.dma_start(out=A.h2stage[t % 2][fc][:, :], in_=A.h2_hbm[t, fc])
                C.inc(ins, "qS", 16, mark=f"q.h2a.{t}" if fc == 7 else None)
        for mch in range(8):
            if C.kind == "tensor":
                if mch == 0:
                    C.wait(f"q.h2a.{t}")
                    C.wait(f"s.cup.{t - 1}.7")
                    if t == 0:
                        for gi in range(3):
                            C.wait(_w_cast_mark(WOFF[["wf_T", "wi_T", "wc_T"][gi]] + 7))
                        for gi in range(3):
                            C.wait(_w_cast_mark(WOFF[["uf_T", "ui_T", "uc_T"][gi]] + 7))
                for gate in range(3):
                    tm = t * 8 + mch
                    if tm >= 2:
                        pt, pm_ = (tm - 2) // 8, (tm - 2) % 8
                        C.wait(f"s.gact.{pt}.{pm_}.{gate}")
                    for ich in range(8):
                        e.matmul(A.pg[gate][mch % 2][:, :],
                                 A.wgb[gate][ich][:, mch * 128:(mch + 1) * 128],
                                 A.h2stage[t % 2][ich][:, :], start=(ich == 0), stop=False)
                    for m2 in range(8):
                        ins = e.matmul(A.pg[gate][mch % 2][:, :],
                                       A.ugb[gate][m2][:, mch * 128:(mch + 1) * 128],
                                       A.c_bf[t % 2][m2][:, :], start=False, stop=(m2 == 7))
                    C.inc(ins, "sT", 1, mark=f"t.g3.{t}.{mch}.{gate}")
                if mch == 7:
                    C.inc(_MockInstr(), "sT", 0, mark=f"t.wdone.{t}")
            elif C.kind == "scalar":
                tm = t * 8 + mch
                if tm >= 2:
                    pt, pm_ = (tm - 2) // 8, (tm - 2) % 8
                    C.wait(f"v.cadd.{pt}.{pm_}")
                for gate, (func, vb) in enumerate([(AF.Sigmoid, 2), (AF.Sigmoid, 3), (AF.Tanh, 4)]):
                    C.wait(f"t.g3.{t}.{mch}.{gate}")
                    ins = e.activation(A.gate_s[mch % 2][gate][:, :], A.pg[gate][mch % 2][:, :], func,
                                       bias=A.vecs_fm[:, vb * 8 + mch:vb * 8 + mch + 1])
                    C.inc(ins, "sS", 1, mark=f"s.gact.{t}.{mch}.{gate}")
            elif C.kind == "vector":
                C.wait(f"s.gact.{t}.{mch}.2")
                gs = A.gate_s[mch % 2]
                C.inc(e.tensor_mul(A.cw[:, :], gs[1][:, :], gs[2][:, :]), "sV", 1, mark=f"v.cw.{t}.{mch}")
                C.inc(e.tensor_mul(A.cw2[:, :], gs[0][:, :], A.c_f32[t % 2][mch][:, :]), "sV", 1,
                      mark=f"v.cw2.{t}.{mch}")
                C.wait(f"v.cw.{t}.{mch}")
                C.wait(f"v.cw2.{t}.{mch}")
                C.inc(e.tensor_tensor(A.c_f32[(t + 1) % 2][mch][:, :], A.cw2[:, :],
                                      A.cw[:, :], ALU.add), "sV", 1, mark=f"v.cadd.{t}.{mch}")
        if C.kind == "scalar":
            for mch in range(8):
                C.wait(f"v.cadd.{t}.{mch}")
                ins = e.activation(A.c_bf[(t + 1) % 2][mch][:, :], A.c_f32[(t + 1) % 2][mch][:, :], AF.Identity)
                C.inc(ins, "sS", 1, mark=f"s.cup.{t}.{mch}")


def emit_p4b(C, A):
    e = C.eng
    cfin = NB % 2
    emit_weight_dmas(C, A, "wo_T")
    emit_weight_dmas(C, A, "uo_T")
    emit_weight_casts(C, A, "wo_T", A.wob)
    emit_weight_casts(C, A, "uo_T", A.uob)
    emit_row_bcast(C, A, "BO", A.BObfm,
                   AP(A.BObfm[:, :].tensor, 0, [[8, 128], [1, 8]]), A.BObc,
                   war_mark=f"v.bcev.B1.{NB - 1}")
    for k in range(4):
        if C.kind == "tensor":
            if k == 0:
                C.wait(_w_cast_mark(WOFF["uo_T"] + 7))
                C.wait(f"s.cup.{NB - 1}.7")
            if k >= 1:
                C.wait(f"v.uoev.{k - 1}")
            for h in range(2):
                for m2 in range(8):
                    ins = e.matmul(A.o_ps[h][:, :],
                                   A.c_bf[cfin][m2][:, k * 128:(k + 1) * 128],
                                   A.uob[m2][:, h * 512:(h + 1) * 512], start=(m2 == 0), stop=(m2 == 7))
            C.inc(ins, "sT", 1, mark=f"t.uo.{k}")
        elif C.kind == "vector":
            C.wait(f"t.uo.{k}")
            C.wait("v.bcev.BO")
            for h in range(2):
                ins = e.tensor_tensor(A.uoterm[k][:, h * 512:(h + 1) * 512], A.o_ps[h][:, :],
                                      A.BObc[:, h * 512:(h + 1) * 512], ALU.add)
                C.inc(ins, "sV", 1, mark=f"v.uoev.{k}" if h == 1 else None)
    for g in range(NB):
        emit_row_bcast(C, A, f"GB.{g}", A.GBbfm,
                       AP(A.GBbfm[:, :].tensor, g, [[64, 128], [8, 8]]), A.GBbc,
                       war_mark=("v.bcev.BO" if g == 0 else f"v.tb2done.{g - 1}"))
        if C.kind == "sync":
            if g >= 2:
                C.wait(f"t.wo.{g - 2}.3")
            for fc in range(8):
                ins = e.dma_start(out=A.h2stage[g % 2][fc][:, :], in_=A.h2_hbm[g, fc])
                C.inc(ins, "qS", 16, mark=f"q.h2b.{g}" if fc == 7 else None)
        for k in range(4):
            kc = k % 2
            gk = g * 4 + k
            if C.kind == "tensor":
                if k == 0:
                    C.wait(f"q.h2b.{g}")
                    if g == 0:
                        C.wait(_w_cast_mark(WOFF["wo_T"] + 7))
                        C.wait("v.uoev.3")
                if gk >= 1:
                    pgg, pk = (gk - 1) // 4, (gk - 1) % 4
                    C.wait(f"v.pre.{pgg}.{pk}")
                for h in range(2):
                    for ich in range(8):
                        ins = e.matmul(A.o_ps[h][:, :],
                                       A.h2stage[g % 2][ich][:, k * 128:(k + 1) * 128],
                                       A.wob[ich][:, h * 512:(h + 1) * 512], start=(ich == 0), stop=(ich == 7))
                C.inc(ins, "sT", 1, mark=f"t.wo.{g}.{k}")
            elif C.kind == "vector":
                C.wait(f"t.wo.{g}.{k}")
                if gk == 0:
                    C.wait("v.uoev.3")
                if gk >= 2:
                    C.wait(f"s.hbf.{g if k >= 2 else g - 1}.{(k - 2) % 4}")
                for h in range(2):
                    ins = e.tensor_tensor(A.pre_t[kc][:, h * 512:(h + 1) * 512], A.o_ps[h][:, :],
                                          A.uoterm[k][:, h * 512:(h + 1) * 512], ALU.add)
                    C.inc(ins, "sV", 1, mark=f"v.pre.{g}.{k}" if h == 1 else None)
                C.wait(f"s.hbf.{g}.{k}")
                C.wait(f"v.bcev.GB.{g}")
                if gk >= 2:
                    C.wait(f"g.oadd.{g if k >= 2 else g - 1}.{(k - 2) % 4}")
                ins = e.tensor_mul(A.tbf2[kc][:, :], A.hbf[kc][:, :], A.GBbc[:, :])
                C.inc(ins, "sV", 1, mark=f"v.tb2.{g}.{k}")
                if k == 3:
                    C.inc(_MockInstr(), "sV", 0, mark=f"v.tb2done.{g}")
            elif C.kind == "scalar":
                C.wait(f"v.pre.{g}.{k}")
                if gk >= 2:
                    C.wait(f"v.tb2.{g if k >= 2 else g - 1}.{(k - 2) % 4}")
                ins = e.activation(A.hbf[kc][:, :], A.pre_t[kc][:, :], AF.Tanh)
                C.inc(ins, "sS", 1, mark=f"s.hbf.{g}.{k}")
            elif C.kind == "sync":
                C.wait(f"g.x1sp.{g}.{k}")
                if gk >= 2:
                    C.wait(f"g.oadd.{g if k >= 2 else g - 1}.{(k - 2) % 4}")
                ins = e.dma_start(out=A.x1stage[kc][:, :], in_=A.x1_hbm[g, k * 128:(k + 1) * 128, :])
                C.inc(ins, "qS", 16, mark=f"q.x1r.{g}.{k}")
            elif C.kind == "gpsimd":
                C.wait(f"v.tb2.{g}.{k}")
                C.wait(f"q.x1r.{g}.{k}")
                ins = e.tensor_tensor(A.outt[kc][:, :], A.x1stage[kc][:, :], A.tbf2[kc][:, :], ALU.add)
                C.inc(ins, "qG", 1, mark=f"g.oadd.{g}.{k}")
                C.wait(f"g.oadd.{g}.{k}")
                ins = e.dma_start(out=A.dram["out"][g, k * 128:(k + 1) * 128, :], in_=A.outt[kc][:, :])
                C.inc(ins, "qG", 16, mark=f"g.out.{g}.{k}")



def emit_fin(C, A):
    if C.kind == "gpsimd":
        C.inc(_MockInstr(), "qG", 0, mark="qG.end")
        C.wait("qG.end")
        C.wait("qS.end")
    elif C.kind == "sync":
        C.inc(_MockInstr(), "qS", 0, mark="qS.end")


def build_program():
    nc = bass.Bass("TRN2", target_bir_lowering=False)
    A = SimpleNamespace()
    A.dram = {}
    A.dram["x_main"] = nc.dram_tensor("x_main", [NB, SLICE, D], F32, kind="ExternalInput").ap()
    A.dram["x_pref"] = nc.dram_tensor("x_pref", [NB, W, D], F32, kind="ExternalInput").ap()
    A.dram["pmask"] = nc.dram_tensor("pmask", [128, W], BF16, kind="ExternalInput").ap()
    A.dram["ident"] = nc.dram_tensor("ident", [128, 128], BF16, kind="ExternalInput").ap()
    A.dram["vecs"] = nc.dram_tensor("vecs", [128, 56], F32, kind="ExternalInput").ap()
    A.dram["c_in"] = nc.dram_tensor("c_in", [128, 64], F32, kind="ExternalInput").ap()
    A.dram["wmod"] = nc.dram_tensor("wmod", [D, 6 * D], F32, kind="ExternalInput").ap()
    for nm in WSEQ:
        A.dram[nm] = nc.dram_tensor(nm, [D, D], F32, kind="ExternalInput").ap()
    A.dram["out"] = nc.dram_tensor("out", [NB, SLICE, D], F32, kind="ExternalOutput").ap()
    scratch_kind = lambda n: "ExternalOutput" if n in TAPS else "Internal"
    A.x1_hbm = nc.dram_tensor("x1_hbm", [NB, SLICE, D], F32, kind=scratch_kind("x1")).ap()
    A.h2_hbm = nc.dram_tensor("h2_hbm", [NB, DCH, 128, SLICE], BF16, kind=scratch_kind("h2")).ap()
    if "mod" in TAPS:
        A.dram["tap_mod"] = nc.dram_tensor("tap_mod", [128, 384], F32, kind="ExternalOutput").ap()
    if "xb" in TAPS:
        A.dram["tap_xb"] = [nc.dram_tensor(f"tap_xb{i}", [128, NB * XBW], BF16, kind="ExternalOutput").ap()
                            for i in range(8)]
        A.dram["tap_xb2"] = [nc.dram_tensor(f"tap_xb2_{i}", [128, NB * XBW], BF16, kind="ExternalOutput").ap()
                             for i in range(8)]
    if "h1" in TAPS:
        A.dram["tap_h1"] = [nc.dram_tensor(f"tap_h1{i}", [128, XBW], BF16, kind="ExternalOutput").ap()
                            for i in range(8)]
    if "H" in TAPS:
        A.dram["tap_H"] = [nc.dram_tensor(f"tap_H{i}", [128, NB * SLICE], BF16, kind="ExternalOutput").ap()
                           for i in range(8)]

    C = Ctx()

    def run_phase(block, emit, A):
        for kind in KINDS:
            C.setup(kind, _MockEng(), True)
            emit(C, A)
        for kind in KINDS:
            def body(eng, kind=kind):
                C.setup(kind, eng, False)
                emit(C, A)
            getattr(block, kind)(body)

    with ExitStack() as st:
        sb = lambda n, s, d: st.enter_context(nc.sbuf_tensor(n, s, d))
        A.ident_b = sb("ident_b", [128, 128], BF16)
        A.vecs_fm = sb("vecs_fm", [128, 56], F32)
        A.c_fm = sb("c_fm", [128, 64], F32)
        A.silu_c = sb("silu_c", [128, 64], F32)
        A.silu_cb = sb("silu_cb", [128, 64], BF16)
        A.mod_fm = sb("mod_fm", [128, 384], F32)
        A.onepca = sb("onepca", [128, 64], F32)
        A.onepcb = sb("onepcb", [128, 64], F32)
        A.G1bfm = sb("G1bfm", [128, 64], BF16)
        A.B1bfm = sb("B1bfm", [128, 64], BF16)
        A.GBbfm = sb("GBbfm", [128, 64], BF16)
        A.BObfm = sb("BObfm", [128, 8], BF16)
        A.ones_b = sb("ones_b", [1, 128], BF16)
        A.pmask_b = sb("pmask_b", [128, W], BF16)
        A.wstage = [sb(f"wstage{i}", [128, D], F32) for i in range(2)]
        A.rowstage = sb("rowstage", [8, 128], BF16)
        A.rowflat = sb("rowflat", [1, D], BF16)
        for s in ["qS", "qG", "sT", "sV", "sS"]:
            C.sems[s] = st.enter_context(nc.semaphore(s))
            C.counts[s] = 0
        block = st.enter_context(nc.Block())

        with ExitStack() as s0:
            sb0 = lambda n, s, d: s0.enter_context(nc.sbuf_tensor(n, s, d))
            A.wmodstage = [sb0(f"wmodstage{i}", [128, 6 * D], F32) for i in range(2)]
            A.wmod_b = [sb0(f"wmod_b{i}", [128, 6 * D], BF16) for i in range(2)]
            A.ga_cl = sb0("ga_cl", [128, 64], F32)
            A.gb_cl = sb0("gb_cl", [128, 64], F32)
            A.cl_m1 = sb0("cl_m1", [128, 64], F32)
            A.cl_m2 = sb0("cl_m2", [128, 64], F32)
            A.cl_mask = sb0("cl_mask", [128, 64], U8)
            A.G1_fm = sb0("G1_fm", [128, 64], F32)
            A.B1_fm = sb0("B1_fm", [128, 64], F32)
            A.mod_ps = [s0.enter_context(nc.psum_tensor(f"mod_ps{i}", [128, 384], F32)) for i in range(2)]
            if "p0" in PHASES:
                run_phase(block, emit_p0, A)

        with ExitStack() as sH:
            A.H_valid = [sH.enter_context(nc.sbuf_tensor(f"Hv{i}", [128, NB * SLICE], BF16)) for i in range(8)]
            with ExitStack() as sC:
                A.C_Tb = [sC.enter_context(nc.sbuf_tensor(f"CT{i}", [128, D], BF16)) for i in range(8)]
                with ExitStack() as sXB:
                    A.xb_b = [sXB.enter_context(nc.sbuf_tensor(f"xb{i}", [128, NB * XBW], BF16)) for i in range(8)]
                    with ExitStack() as s1:
                        sb1 = lambda n, s, d: s1.enter_context(nc.sbuf_tensor(n, s, d))
                        A.B_Tb = [sb1(f"BT{i}", [128, D], BF16) for i in range(8)]
                        A.h1_fm = [sb1(f"h1_{i}", [128, XBW], BF16) for i in range(8)]
                        A.xstage = [sb1(f"xst{i}", [128, D], F32) for i in range(2)]
                        A.xh_bf = [sb1(f"xh{i}", [128, D], BF16) for i in range(2)]
                        A.s_sum = sb1("s_sum", [128, 2], F32)
                        A.s_mneg = sb1("s_mneg", [128, 2], F32)
                        A.s_ssq = sb1("s_ssq", [128, 2], F32)
                        A.s_var = sb1("s_var", [128, 2], F32)
                        A.s_std = sb1("s_std", [128, 2], F32)
                        A.s_rstd = sb1("s_rstd", [128, 2], F32)
                        A.s_bias = sb1("s_bias", [128, 2], F32)
                        A.h1mm = A.H_valid
                        A.h1stage = sb1("h1stage", [128, 512], BF16)
                        A.h1junk = sb1("h1junk", [128, 128], BF16)
                        A.tr_ps = [s1.enter_context(nc.psum_tensor(f"trp{i}", [128, 512], BF16)) for i in range(2)]
                        A.dl_ps = s1.enter_context(nc.psum_tensor("dl_ps", [128, 128], F32))
                        A.xb_ps = [s1.enter_context(nc.psum_tensor(f"xbp{i}", [128, 512], F32)) for i in range(4)]
                        if "p1" in PHASES:
                            run_phase(block, emit_p1, A)
                    with ExitStack() as s2:
                        sb2 = lambda n, s, d: s2.enter_context(nc.sbuf_tensor(n, s, d))
                        A.A_Tb = [sb2(f"AT{i}", [128, D], BF16) for i in range(8)]
                        A.h_par = [[sb2(f"hp{p}_{i}", [128, 128], BF16) for i in range(8)] for p in range(2)]
                        A.rec_tmp = [[sb2(f"rt{p}_{i}", [128, 128], F32) for i in range(8)] for p in range(2)]
                        A.pm = [s2.enter_context(nc.psum_tensor(f"pm{i}", [128, 128], F32)) for i in range(8)]
                        if "p2" in PHASES:
                            run_phase(block, emit_p2, A)
                with ExitStack() as s3:
                    sb3 = lambda n, s, d: s3.enter_context(nc.sbuf_tensor(n, s, d))
                    A.x3stage = [sb3(f"x3_{i}", [128, D], F32) for i in range(2)]
                    A.out1 = [sb3(f"o1_{i}", [128, D], F32) for i in range(2)]
                    A.u_t = [sb3(f"u_{i}", [128, D], F32) for i in range(2)]
                    A.x1t = [sb3(f"x1_{i}", [128, D], F32) for i in range(2)]
                    A.xh3a = [sb3(f"xh3a{i}", [128, D], BF16) for i in range(2)]
                    A.xh3b = [sb3(f"xh3b{i}", [128, D], BF16) for i in range(2)]
                    A.tbf = [sb3(f"tbf{i}", [128, D], BF16) for i in range(2)]
                    A.G1bc = sb3("G1bc", [128, D], BF16)
                    A.B1bc = sb3("B1bc", [128, D], BF16)
                    A.h2cur = [sb3(f"h2c{i}", [128, SLICE], BF16) for i in range(8)]
                    for nm in ["s3_sum", "s3_mneg", "s3_ssq", "s3_var", "s3_std", "s3_rstd", "s3_bias",
                               "s3_sum2", "s3_mneg2", "s3_ssq2", "s3_var2", "s3_std2", "s3_rstd2", "s3_bias2"]:
                        setattr(A, nm, sb3(nm, [128, 2], F32))
                    A.y_ps2 = [[s3.enter_context(nc.psum_tensor(f"yp{i}_{h}", [128, 512], F32)) for h in range(2)]
                               for i in range(2)]
                    A.tr_ps2 = s3.enter_context(nc.psum_tensor("trq", [128, 512], BF16))
                    A.rowT_ps = s3.enter_context(nc.psum_tensor("rowT_ps", [8, 128], BF16))
                    A.bc_ps = s3.enter_context(nc.psum_tensor("bcp", [128, 1024], F32))
                    if "p3" in PHASES:
                        run_phase(block, emit_p3, A)

        with ExitStack() as s4:
            sb4 = lambda n, s, d: s4.enter_context(nc.sbuf_tensor(n, s, d))
            A.h2stage = [[sb4(f"h2s{p}_{i}", [128, SLICE], BF16) for i in range(8)] for p in range(2)]
            A.c_f32 = [[sb4(f"cf{p}_{i}", [128, SLICE], F32) for i in range(8)] for p in range(2)]
            A.c_bf = [[sb4(f"cb{p}_{i}", [128, SLICE], BF16) for i in range(8)] for p in range(2)]
            with ExitStack() as s4a:
                sba = lambda n, s, d: s4a.enter_context(nc.sbuf_tensor(n, s, d))
                A.wgb = [[sba(f"wg{gi}_{i}", [128, D], BF16) for i in range(8)] for gi in range(3)]
                A.ugb = [[sba(f"ug{gi}_{i}", [128, D], BF16) for i in range(8)] for gi in range(3)]
                A.gate_s = [[sba(f"gs{p}_{gi}", [128, SLICE], F32) for gi in range(3)] for p in range(2)]
                A.cw = sba("cw", [128, SLICE], F32)
                A.cw2 = sba("cw2", [128, SLICE], F32)
                A.ones512 = sba("ones512", [128, SLICE], F32)
                A.pg = [[s4a.enter_context(nc.psum_tensor(f"pg{gi}_{p}", [128, 512], F32)) for p in range(2)]
                        for gi in range(3)]
                if "p4a" in PHASES:
                    run_phase(block, emit_p4a, A)
            with ExitStack() as s4b:
                sbb = lambda n, s, d: s4b.enter_context(nc.sbuf_tensor(n, s, d))
                A.wob = [sbb(f"wob{i}", [128, D], BF16) for i in range(8)]
                A.uob = [sbb(f"uob{i}", [128, D], BF16) for i in range(8)]
                A.uoterm = [sbb(f"uot{i}", [128, D], F32) for i in range(4)]
                A.GBbc = sbb("GBbc", [128, D], BF16)
                A.BObc = sbb("BObc", [128, D], BF16)
                A.x1stage = [sbb(f"x1s{i}", [128, D], F32) for i in range(2)]
                A.pre_t = [sbb(f"pre{i}", [128, D], F32) for i in range(2)]
                A.hbf = [sbb(f"hbf{i}", [128, D], BF16) for i in range(2)]
                A.tbf2 = [sbb(f"tbf2{i}", [128, D], BF16) for i in range(2)]
                A.outt = [sbb(f"outt{i}", [128, D], F32) for i in range(2)]
                A.o_ps = [s4b.enter_context(nc.psum_tensor(f"op{h}", [128, 512], F32)) for h in range(2)]
                A.rowT_ps = s4b.enter_context(nc.psum_tensor("rowT_ps2", [8, 128], BF16))
                A.bc_ps = s4b.enter_context(nc.psum_tensor("bcq", [128, 1024], F32))
                if "p4b" in PHASES:
                    run_phase(block, emit_p4b, A)

        run_phase(block, emit_fin, A)
        assert C.counts == C.counts_real, f"count drift: {C.counts} vs {C.counts_real}"

    return nc


def _pack_vecs(vals):
    cols = []
    for v in vals:
        cols.append(np.ascontiguousarray(v.reshape(8, 128).T))
    return np.ascontiguousarray(np.concatenate(cols, axis=1), dtype=np.float32)


def kernel(L0_attn_x, L0_attn_A, L0_attn_B, L0_attn_C, L0_attn_gamma, L0_attn_beta,
           L0_ffn_wf, L0_ffn_uf, L0_ffn_bf, L0_ffn_wi, L0_ffn_ui, L0_ffn_bi,
           L0_ffn_wc, L0_ffn_uc, L0_ffn_bc, L0_ffn_wo, L0_ffn_uo, L0_ffn_bo,
           L0_ffn_in_c, L0_c, L0_wmod):
    x = np.ascontiguousarray(np.asarray(L0_attn_x, dtype=np.float32))
    nc = build_program()

    tr = lambda a: np.ascontiguousarray(np.asarray(a, dtype=np.float32).T)
    shared = {
        "ident": np.eye(128, dtype=ml_dtypes.bfloat16),
        "vecs": _pack_vecs([np.asarray(v, np.float32) for v in
                            [L0_attn_gamma, L0_attn_beta, L0_ffn_bf, L0_ffn_bi, L0_ffn_bc, L0_ffn_bo,
                             L0_ffn_in_c]]),
        "c_in": np.ascontiguousarray(
            np.asarray(L0_c, np.float32).T.reshape(8, 128, 8).transpose(1, 0, 2).reshape(128, 64)),
        "wmod": np.ascontiguousarray(np.asarray(L0_wmod, np.float32)),
        "A_T": tr(L0_attn_A), "B_T": tr(L0_attn_B), "C_T": tr(L0_attn_C),
        "wf_T": tr(L0_ffn_wf), "wi_T": tr(L0_ffn_wi), "wc_T": tr(L0_ffn_wc),
        "uf_T": tr(L0_ffn_uf), "ui_T": tr(L0_ffn_ui), "uc_T": tr(L0_ffn_uc),
        "wo_T": tr(L0_ffn_wo), "uo_T": tr(L0_ffn_uo),
    }
    in_maps = []
    for c in range(8):
        m = dict(shared)
        m["x_main"] = np.ascontiguousarray(x[:, c * SLICE:(c + 1) * SLICE, :])
        if c == 0:
            m["x_pref"] = np.zeros((NB, W, D), np.float32)
            m["pmask"] = np.zeros((128, W), dtype=ml_dtypes.bfloat16)
        else:
            m["x_pref"] = np.ascontiguousarray(x[:, c * SLICE - W:c * SLICE, :])
            m["pmask"] = np.ones((128, W), dtype=ml_dtypes.bfloat16)
        in_maps.append(m)

    res = run_bass_kernel_spmd(nc, in_maps, core_ids=list(range(8)))
    out = np.concatenate([r["out"] for r in res.results], axis=1)
    if TAPS:
        kernel.last_results = res
    return out


# version 05
# speedup vs baseline: 1.0015x; 1.0015x over previous
import numpy as np
import ml_dtypes
import concourse.bass as bass
import concourse.mybir as mybir
from concourse.ap import AP
from concourse.bass_utils import run_bass_kernel_spmd
from contextlib import ExitStack
from types import SimpleNamespace

F32 = mybir.dt.float32
BF16 = mybir.dt.bfloat16
U8 = mybir.dt.uint8
AF = mybir.ActivationFunctionType
ALU = mybir.AluOpType
AXX = mybir.AxisListType.X

NB = 8
D = 1024
DCH = 8
SLICE = 512
L = 32
W = 80
NSTEP = W + L
NCHK = SLICE // L
XBW = SLICE + W
T1OFF = [0, W, W + 128, W + 256, W + 384]
T1N = [W, 128, 128, 128, 128]
NT1 = 5
EPS = 1e-5

KINDS = ["sync", "gpsimd", "tensor", "vector", "scalar"]
WSEQ = ["B_T", "A_T", "C_T", "wf_T", "wi_T", "wc_T", "uf_T", "ui_T", "uc_T", "wo_T", "uo_T"]
WOFF = {n: 8 * i for i, n in enumerate(WSEQ)}
WCASTER = {n: "scalar" for n in
           ["B_T", "A_T", "C_T", "wf_T", "wi_T", "wc_T", "uf_T", "ui_T", "uc_T", "wo_T", "uo_T"]}

TAPS = set()
PHASES = ["p0", "p1", "p2", "p3", "p4a", "p4b"]


class _MockInstr:
    def then_inc(self, *a, **k):
        return self


class _MockEng:
    def __getattr__(self, name):
        def f(*a, **k):
            return _MockInstr()
        return f

    def wait_ge(self, *a, **k):
        pass


class Ctx:
    def __init__(self):
        self.marks = {}
        self.counts = {}
        self.counts_real = {}
        self.sems = {}
        self.dry = True
        self.kind = None
        self.eng = None

    def setup(self, kind, eng, dry):
        self.kind = kind
        self.eng = eng
        self.dry = dry

    def inc(self, instr, semname, n=1, mark=None):
        if self.dry:
            self.counts[semname] = self.counts.get(semname, 0) + n
            if mark is not None:
                assert mark not in self.marks, f"dup mark {mark}"
                self.marks[mark] = (semname, self.counts[semname])
        else:
            self.counts_real[semname] = self.counts_real.get(semname, 0) + n
            if mark is not None:
                assert self.marks.get(mark) == (semname, self.counts_real[semname]), (
                    f"mark drift {mark}: dry {self.marks.get(mark)} real "
                    f"({semname}, {self.counts_real[semname]})")
        instr.then_inc(self.sems[semname], n)

    def wait(self, mark):
        if self.dry:
            return
        semname, v = self.marks[mark]
        self.eng.wait_ge(self.sems[semname], v)

    def wait_if(self, mark):
        if self.dry:
            return
        if mark in self.marks:
            semname, v = self.marks[mark]
            self.eng.wait_ge(self.sems[semname], v)


def _w_dma_mark(i):
    return f"q.w.{i}"


def _w_cast_mark(i):
    return f"c.w.{i}"


def emit_weight_dmas(C, A, name):
    if C.kind != "sync":
        return
    base = WOFF[name]
    for ch in range(8):
        i = base + ch
        if i >= 2:
            C.wait(_w_cast_mark(i - 2))
        ins = C.eng.dma_start(out=A.wstage[i % 2][:, :], in_=A.dram[name][ch * 128:(ch + 1) * 128, :])
        C.inc(ins, f"qW{i % 2}", 16, mark=_w_dma_mark(i))


def emit_weight_casts(C, A, name, dest):
    if C.kind != WCASTER[name]:
        return
    base = WOFF[name]
    for ch in range(8):
        i = base + ch
        C.wait(_w_dma_mark(i))
        ins = C.eng.activation(dest[ch][:, :], A.wstage[i % 2][:, :], AF.Identity)
        C.inc(ins, "sS", 1, mark=_w_cast_mark(i))


def emit_row_bcast(C, A, tag, src_bf, col_view, bc_dests, war_mark=None):
    if C.kind == "tensor":
        if war_mark is not None:
            C.wait(war_mark)
        C.wait("v.g1fm")
        ins = C.eng.transpose(A.rowT_ps[:, :], col_view, A.ident_b[:, :])
        C.inc(ins, "sT", 1, mark=f"t.rowT.{tag}")
        C.wait(f"q.rowflat.{tag}")
        for h in range(2):
            ins = C.eng.matmul(A.bc_ps[:, h * 512:(h + 1) * 512], A.ones_b[0:1, :],
                               A.rowflat[0:1, h * 512:(h + 1) * 512], start=True, stop=True)
        C.inc(ins, "sT", 1, mark=f"t.bc.{tag}")
    elif C.kind == "vector":
        C.wait(f"t.rowT.{tag}")
        ins = C.eng.tensor_copy(A.rowstage[:, :], A.rowT_ps[:, :])
        C.inc(ins, "sV", 1, mark=f"v.rowev.{tag}")
        C.wait(f"t.bc.{tag}")
        ins = C.eng.tensor_copy(bc_dests[:, :], A.bc_ps[:, :])
        C.inc(ins, "sV", 1, mark=f"v.bcev.{tag}")
    elif C.kind == "sync":
        C.wait(f"v.rowev.{tag}")
        ins = C.eng.dma_start(out=A.rowflat[0:1, :], in_=A.rowstage[:, :])
        C.inc(ins, "qR", 16, mark=f"q.rowflat.{tag}")


def xb_view(A, e, t):
    return AP(A.xb_b[e][:, :].tensor, t, [[NB * XBW, 128], [XBW, NB], [L, NCHK]])


def hv_view(A, e, off):
    return AP(A.H_valid[e][:, :].tensor, off, [[NB * SLICE, 128], [SLICE, NB], [L, NCHK]])


def emit_p0(C, A):
    if C.kind == "sync":
        for ci, (nm, dst) in enumerate([("ident", A.ident_b), ("vecs", A.vecs_fm), ("c_in", A.c_fm),
                                        ("pmask", A.pmask_b)]):
            ins = C.eng.dma_start(out=dst[:, :], in_=A.dram[nm])
            C.inc(ins, f"qC{ci}", 16, mark=f"q.{nm}")
        for d in range(8):
            if d >= 2:
                C.wait(f"c.wmod.{d - 2}")
            ins = C.eng.dma_start(out=A.wmodstage[d % 2][:, :], in_=A.dram["wmod"][d * 128:(d + 1) * 128, :])
            C.inc(ins, f"qM{d % 2}", 16, mark=f"q.wmod.{d}")
    elif C.kind == "scalar":
        C.wait("q.c_in")
        ins = C.eng.activation(A.silu_c[:, :], A.c_fm[:, :], AF.Silu)
        C.inc(ins, "sS", 1, mark="s.silu0")
        C.wait("s.silu0")
        ins = C.eng.activation(A.silu_cb[:, :], A.silu_c[:, :], AF.Identity)
        C.inc(ins, "sS", 1, mark="s.silu")
    elif C.kind == "tensor":
        C.wait("s.silu")
        for d in range(8):
            C.wait(f"c.wmod.{d}")
            if d >= 2:
                C.wait(f"v.modev.{d - 2}")
            for mch in range(48):
                ins = C.eng.matmul(A.mod_ps[d % 2][:, mch * 8:(mch + 1) * 8],
                                   A.wmod_b[d % 2][:, mch * 128:(mch + 1) * 128],
                                   A.silu_cb[:, d * 8:(d + 1) * 8],
                                   start=True, stop=True)
            C.inc(ins, "sT", 1, mark=f"t.mod.{d}")
    if C.kind == "scalar":
        for d in range(8):
            C.wait(f"q.wmod.{d}")
            if d >= 2:
                C.wait(f"t.mod.{d - 2}")
            ins = C.eng.activation(A.wmod_b[d % 2][:, :], A.wmodstage[d % 2][:, :], AF.Identity)
            C.inc(ins, "sS", 1, mark=f"c.wmod.{d}")
    if C.kind == "gpsimd":
        if "mod" in TAPS:
            C.wait("v.g1fm")
            ins = C.eng.dma_start(out=A.dram["tap_mod"], in_=A.mod_fm[:, :])
            C.inc(ins, "qGO", 16)
    elif C.kind == "vector":
        ins = C.eng.memset(A.ones_b[0:1, :], 1.0)
        C.inc(ins, "sV", 1)
        ins = C.eng.memset(A.mod_fm[:, :], 0.0)
        C.inc(ins, "sV", 1, mark="v.mod0")
        for d in range(8):
            C.wait(f"t.mod.{d}")
            C.wait("v.mod0" if d == 0 else f"v.modev.{d - 1}")
            ins = C.eng.tensor_tensor(A.mod_fm[:, :], A.mod_fm[:, :], A.mod_ps[d % 2][:, :], ALU.add)
            C.inc(ins, "sV", 1, mark=f"v.modev.{d}")
        C.wait("v.modev.7")
        C.inc(_MockInstr(), "sV", 0, mark="v.mod")
        e = C.eng
        C.inc(e.tensor_scalar_add(A.onepca[:, :], A.mod_fm[:, 64:128], 1.0), "sV", 1)
        C.inc(e.tensor_scalar_add(A.onepcb[:, :], A.mod_fm[:, 256:320], 1.0), "sV", 1)
        for ci, (src, dst) in enumerate([(A.mod_fm[:, 128:192], A.ga_cl), (A.mod_fm[:, 320:384], A.gb_cl)]):
            C.inc(e.tensor_scalar_max(A.cl_m1[:, :], src, 0.25), "sV", 1)
            C.inc(e.tensor_scalar_min(A.cl_m2[:, :], src, -0.25), "sV", 1)
            C.inc(e.tensor_single_scalar(A.cl_mask[:, :], src, 0.0, ALU.is_ge), "sV", 1, mark=f"v.clin.{ci}")
            C.wait(f"v.clin.{ci}")
            C.inc(e.select(dst[:, :], A.cl_mask[:, :], A.cl_m1[:, :], A.cl_m2[:, :]), "sV", 1, mark=f"v.clsel.{ci}")
        C.wait("q.vecs")
        C.wait("v.clsel.1")
        for fc in range(8):
            gcol = A.vecs_fm[:, fc:fc + 1].broadcast_to((128, 8))
            bcol = A.vecs_fm[:, 8 + fc:8 + fc + 1].broadcast_to((128, 8))
            C.inc(e.tensor_mul(A.G1_fm[:, fc * 8:(fc + 1) * 8], A.ga_cl[:, fc * 8:(fc + 1) * 8], gcol), "sV", 1)
            ins = e.tensor_mul(A.B1_fm[:, fc * 8:(fc + 1) * 8], A.ga_cl[:, fc * 8:(fc + 1) * 8], bcol)
            C.inc(ins, "sV", 1, mark=f"v.g1prod.{fc}" if fc == 7 else None)
        C.wait("v.g1prod.7")
        C.inc(_MockInstr(), "sV", 0, mark="v.g1f")
    if C.kind == "scalar":
        e2 = C.eng
        C.wait("v.g1f")
        C.inc(e2.activation(A.G1bfm[:, :], A.G1_fm[:, :], AF.Identity), "sS", 1)
        C.inc(e2.activation(A.B1bfm[:, :], A.B1_fm[:, :], AF.Identity), "sS", 1)
        C.inc(e2.activation(A.GBbfm[:, :], A.gb_cl[:, :], AF.Identity), "sS", 1)
        C.wait("q.vecs")
        C.inc(e2.activation(A.BObfm[:, :], A.vecs_fm[:, 40:48], AF.Identity), "sS", 1, mark="v.g1fm")


def emit_p1(C, A):
    e = C.eng
    for g in range(NB):
        if C.kind == "sync":
            if g == 0:
                emit_weight_dmas(C, A, "B_T")
            for i in range(NT1):
                gt = g * NT1 + i
                if gt >= 2:
                    C.wait(f"s.xhat.{(gt - 2) // NT1}.{(gt - 2) % NT1}")
                n = T1N[i]
                src = A.dram["x_pref"][g, :, :] if i == 0 else A.dram["x_main"][g, (i - 1) * 128:i * 128, :]
                ins = e.dma_start(out=A.xstage[gt % 2][0:n, :], in_=src)
                C.inc(ins, f"qX{gt % 2}", 16, mark=f"q.x1.{g}.{i}")
        if g == 0:
            emit_weight_casts(C, A, "B_T", A.B_Tb)
        for i in range(NT1):
            n = T1N[i]
            st = A.xstage[(g * NT1 + i) % 2]
            xh = A.xh_bf[(g * NT1 + i) % 2]
            ic = i % 2
            if C.kind == "vector":
                C.wait(f"q.x1.{g}.{i}")
                C.inc(e.reduce_sum(A.s_sum[0:n, ic:ic + 1], st[0:n, :], axis=AXX), "sV", 1,
                      mark=f"v.rsum.{g}.{i}")
                C.wait(f"v.rsum.{g}.{i}")
                C.inc(e.tensor_scalar_mul(A.s_mneg[0:n, ic:ic + 1], A.s_sum[0:n, ic:ic + 1], -1.0 / D),
                      "sV", 1, mark=f"v.mu.{g}.{i}")
                C.wait(f"s.sq.{g}.{i}")
                C.inc(e.tensor_scalar(A.s_var[0:n, ic:ic + 1], A.s_ssq[0:n, ic:ic + 1],
                                      scalar1=1.0 / D, scalar2=EPS, op0=ALU.mult, op1=ALU.add),
                      "sV", 1, mark=f"v.var.{g}.{i}")
                C.wait(f"s.sqrt.{g}.{i}")
                C.inc(e.reciprocal(A.s_rstd[0:n, ic:ic + 1], A.s_std[0:n, ic:ic + 1]), "sV", 1,
                      mark=f"v.rcp.{g}.{i}")
                C.wait(f"v.rcp.{g}.{i}")
                C.inc(e.tensor_mul(A.s_bias[0:n, ic:ic + 1], A.s_mneg[0:n, ic:ic + 1], A.s_rstd[0:n, ic:ic + 1]),
                      "sV", 1, mark=f"v.rstd.{g}.{i}")
            elif C.kind == "scalar":
                C.wait(f"v.mu.{g}.{i}")
                gt = g * NT1 + i
                if gt >= 2:
                    pg, pi = (gt - 2) // NT1, (gt - 2) % NT1
                    C.wait(f"t.tr.{pg}.{pi}.1")
                C.inc(e.activation(xh[0:n, :], st[0:n, :], AF.Square,
                                   bias=A.s_mneg[0:n, ic:ic + 1], accum_out=A.s_ssq[0:n, ic:ic + 1]),
                      "sS", 1, mark=f"s.sq.{g}.{i}")
                C.wait(f"v.var.{g}.{i}")
                C.inc(e.sqrt(A.s_std[0:n, ic:ic + 1], A.s_var[0:n, ic:ic + 1]), "sS", 1, mark=f"s.sqrt.{g}.{i}")
                C.wait(f"v.rstd.{g}.{i}")
                C.inc(e.activation(xh[0:n, :], st[0:n, :], AF.Identity,
                                   bias=A.s_bias[0:n, ic:ic + 1], scale=A.s_rstd[0:n, ic:ic + 1]),
                      "sS", 1, mark=f"s.xhat.{g}.{i}")
            if C.kind == "tensor":
                C.wait(f"s.xhat.{g}.{i}")
                C.wait("v.g1fm")
                for half in range(2):
                    gt = g * NT1 + i
                    if gt >= 1:
                        pg, pi = (gt - 1) // NT1, (gt - 1) % NT1
                        C.wait(f"s.h1ev.{pg}.{pi}.{half}")
                    for fq in range(4):
                        fc = half * 4 + fq
                        ins = e.transpose(A.tr_ps[half][:, fq * 128:fq * 128 + n],
                                          xh[0:n, fc * 128:(fc + 1) * 128], A.ident_b[0:n, 0:n])
                    C.inc(ins, "sT", 1, mark=f"t.tr.{g}.{i}.{half}")
            if C.kind == "scalar":
                C.wait("v.g1f")
                for half in range(2):
                    C.wait(f"t.tr.{g}.{i}.{half}")
                    if g >= 1 and i == 0 and half == 0:
                        C.wait(f"v.xbev.{g - 1}.3.3")
                        C.wait(f"v.xbm.{g - 1}")
                    ins = e.activation(A.h1stage[:, :], A.tr_ps[half][:, :], AF.Identity)
                    C.inc(ins, "sS", 1, mark=f"s.h1st.{g}.{i}.{half}")
                    C.wait(f"s.h1st.{g}.{i}.{half}")
                    for fq in range(4):
                        fc = half * 4 + fq
                        sac = A.mod_fm[:, fc * 8 + g:fc * 8 + g + 1]
                        ins = e.activation(A.h1_fm[fc][:, T1OFF[i]:T1OFF[i] + n],
                                           A.h1stage[:, fq * 128:fq * 128 + n], AF.Identity,
                                           scale=A.onepca[:, fc * 8 + g:fc * 8 + g + 1], bias=sac)
                        C.inc(ins, "sS", 1, mark=f"s.h1ev.{g}.{i}.{half}" if fq == 3 else None)
        if C.kind == "scalar":
            ins = e.activation(A.h1stage[:, :], A.h1stage[:, :], AF.Identity)
            C.inc(ins, "sS", 1)
            ins = e.activation(A.h1stage[:, :], A.h1stage[:, :], AF.Identity)
            C.inc(ins, "sS", 1, mark=f"s.h1.{g}")

        CR = [(0, 512), (512, XBW)]
        if C.kind == "vector":
            pass
        for wv in range(4):
            eg, cr = wv % 2, wv // 2
            c0, c1 = CR[cr]
            for el in range(4):
                ee = eg * 4 + el
                if C.kind == "tensor":
                    if wv == 0 and el == 0:
                        C.wait(f"s.h1.{g}")
                        if g == 0:
                            C.wait(_w_cast_mark(WOFF["B_T"] + 7))
                    if g > 0 or wv > 0:
                        pw, pgg = wv - 1, g
                        if pw < 0:
                            pw, pgg = 3, g - 1
                        C.wait(f"v.xbev.{pgg}.{pw}.{el}")
                    for d in range(8):
                        ins = e.matmul(A.xb_ps[el][:, 0:c1 - c0],
                                       A.B_Tb[d][:, ee * 128:(ee + 1) * 128],
                                       A.h1_fm[d][:, c0:c1], start=(d == 0), stop=(d == 7))
                    C.inc(ins, "sT", 1, mark=f"t.xb.{g}.{wv}.{el}")
                elif C.kind == "vector":
                    C.wait(f"t.xb.{g}.{wv}.{el}")
                    ins = e.tensor_copy(A.xb_b[ee][:, g * XBW + c0:g * XBW + c1], A.xb_ps[el][:, 0:c1 - c0])
                    C.inc(ins, "sV", 1, mark=f"v.xbev.{g}.{wv}.{el}")
        if C.kind == "vector":
            C.wait("q.pmask")
            C.wait(f"v.xbev.{g}.1.3")
            for ee in range(8):
                ins = e.tensor_mul(A.xb_b[ee][:, g * XBW:g * XBW + W],
                                   A.xb_b[ee][:, g * XBW:g * XBW + W], A.pmask_b[:, :])
                C.inc(ins, "sV", 1, mark=f"v.xbm.{g}" if ee == 7 else None)
    if C.kind == "gpsimd" and "xb" in TAPS:
        C.wait(f"v.xbm.{NB - 1}")
        for ee in range(8):
            ins = e.dma_start(out=A.dram["tap_xb"][ee], in_=A.xb_b[ee][:, :])
            C.inc(ins, "qGO", 16)
        for ee in range(8):
            ins = e.dma_start(out=A.dram["tap_xb2"][ee], in_=A.xb_b[ee][:, :])
            C.inc(ins, "qGO", 16)
    if C.kind == "gpsimd" and "h1" in TAPS:
        C.wait(f"v.xbm.{NB - 1}")
        for fc in range(8):
            ins = e.dma_start(out=A.dram["tap_h1"][fc], in_=A.h1_fm[fc][:, :])
            C.inc(ins, "qGO", 16)


def emit_p2(C, A):
    e = C.eng
    if C.kind == "sync":
        emit_weight_dmas(C, A, "A_T")
        emit_weight_dmas(C, A, "C_T")
    emit_weight_casts(C, A, "A_T", A.A_Tb)
    if C.kind == "vector":
        C.wait(f"v.xbm.{NB - 1}")
        for d in range(8):
            ins = e.memset(A.h_par[0][d][:, :], 0.0)
            C.inc(ins, "sV", 1, mark="v.h0" if d == 7 else None)
    emit_weight_casts(C, A, "C_T", A.C_Tb)
    for t in range(NSTEP):
        if C.kind == "tensor":
            if t == 0:
                C.wait("v.h0")
            else:
                C.wait(f"s.rec.{t - 1}.7")
            for ee in range(8):
                for d in range(8):
                    if t == 0:
                        rhs = A.h_par[0][d][:, :]
                    elif t <= W:
                        rhs = A.h_par[t % 2][d][:, :]
                    else:
                        rhs = hv_view(A, d, t - 1 - W)
                    ins = e.matmul(A.pm[ee][:, :], A.A_Tb[d][:, ee * 128:(ee + 1) * 128],
                                   rhs, start=(d == 0), stop=(d == 7))
                C.inc(ins, "sT", 1, mark=f"t.rec.{t}.{ee}")
        elif C.kind == "vector":
            if t >= 2:
                C.wait(f"s.rec.{t - 2}.7")
            for ee in range(8):
                C.wait(f"t.rec.{t}.{ee}")
                ins = e.tensor_tensor(A.rec_tmp[t % 2][ee][:, :], A.pm[ee][:, :], xb_view(A, ee, t), ALU.add)
                C.inc(ins, "sV", 1, mark=f"v.rec.{t}.{ee}")
        elif C.kind == "scalar":
            for ee in range(8):
                C.wait(f"v.rec.{t}.{ee}")
                if t < W:
                    dst = A.h_par[(t + 1) % 2][ee][:, :]
                else:
                    dst = hv_view(A, ee, t - W)
                ins = e.activation(dst, A.rec_tmp[t % 2][ee][:, :], AF.Tanh)
                C.inc(ins, "sS", 1, mark=f"s.rec.{t}.{ee}")
    if C.kind == "gpsimd" and "H" in TAPS:
        C.wait(f"s.rec.{NSTEP - 1}.7")
        for ee in range(8):
            ins = e.dma_start(out=A.dram["tap_H"][ee], in_=A.H_valid[ee][:, :])
            C.inc(ins, "qGO", 16)


def emit_p3(C, A):
    e = C.eng
    if C.kind == "tensor":
        C.wait(f"s.rec.{NSTEP - 1}.7")
        C.wait(_w_cast_mark(WOFF["C_T"] + 7))
    for g in range(NB):
        emit_row_bcast(C, A, f"G1.{g}", A.G1bfm,
                       AP(A.G1bfm[:, :].tensor, g, [[64, 128], [8, 8]]), A.G1bc,
                       war_mark=f"v.bcev.B1.{g - 1}" if g > 0 else None)
        emit_row_bcast(C, A, f"B1.{g}", A.B1bfm,
                       AP(A.B1bfm[:, :].tensor, g, [[64, 128], [8, 8]]), A.B1bc,
                       war_mark=f"v.bcev.G1.{g}")
        for k in range(4):
            kc = k % 2
            gk = g * 4 + k
            if C.kind == "sync":
                if gk >= 2:
                    C.wait(f"v.out1.{g if k >= 2 else g - 1}.{(k - 2) % 4}")
                    C.wait(f"p.u.{g if k >= 2 else g - 1}.{(k - 2) % 4}")
                ins = e.dma_start(out=A.x3stage[kc][:, :], in_=A.dram["x_main"][g, k * 128:(k + 1) * 128, :])
                C.inc(ins, f"qX3_{k % 2}", 16, mark=f"q.x3.{g}.{k}")
            elif C.kind == "tensor":
                if gk >= 2:
                    C.wait(f"v.out1.{g if k >= 2 else g - 1}.{(k - 2) % 4}")
                for h in range(2):
                    for ee in range(8):
                        ins = e.matmul(A.y_ps2[kc][h][:, :],
                                       A.H_valid[ee][:, g * SLICE + k * 128:g * SLICE + (k + 1) * 128],
                                       A.C_Tb[ee][:, h * 512:(h + 1) * 512],
                                       start=(ee == 0), stop=(ee == 7))
                C.inc(ins, "sT", 1, mark=f"t.y.{g}.{k}")
            if C.kind == "vector":
                C.wait(f"q.x3.{g}.{k}")
                C.wait(f"t.y.{g}.{k}")
                if gk >= 2:
                    C.wait(f"s.xhat3a.{g if k >= 2 else g - 1}.{(k - 2) % 4}")
                for h in range(2):
                    ins = e.tensor_tensor(A.out1[kc][:, h * 512:(h + 1) * 512], A.y_ps2[kc][h][:, :],
                                          A.x3stage[kc][:, h * 512:(h + 1) * 512], ALU.add)
                    C.inc(ins, "sV", 1, mark=f"v.out1.{g}.{k}" if h == 1 else None)
            elif C.kind == "gpsimd":
                C.wait(f"q.x3.{g}.{k}")
                C.wait(f"v.bcev.B1.{g}")
                if gk >= 2:
                    C.wait(f"v.x1.{g if k >= 2 else g - 1}.{(k - 2) % 4}")
                ins = e.tensor_tensor(A.u_t[kc][:, :], A.x3stage[kc][:, :], A.B1bc[:, :], ALU.add)
                C.inc(ins, "qP", 1, mark=f"p.u.{g}.{k}")
            if C.kind == "vector":
                C.wait(f"v.out1.{g}.{k}")
                C.inc(e.reduce_sum(A.s3_sum[:, kc:kc + 1], A.out1[kc][:, :], axis=AXX), "sV", 1,
                      mark=f"v.rs3.{g}.{k}")
                C.wait(f"v.rs3.{g}.{k}")
                C.inc(e.tensor_scalar_mul(A.s3_mneg[:, kc:kc + 1], A.s3_sum[:, kc:kc + 1], -1.0 / D),
                      "sV", 1, mark=f"v.mu3.{g}.{k}")
                C.wait(f"s.sq3.{g}.{k}")
                C.inc(e.tensor_scalar(A.s3_var[:, kc:kc + 1], A.s3_ssq[:, kc:kc + 1],
                                      scalar1=1.0 / D, scalar2=EPS, op0=ALU.mult, op1=ALU.add),
                      "sV", 1, mark=f"v.var3.{g}.{k}")
                C.wait(f"s.sqrt3.{g}.{k}")
                C.inc(e.reciprocal(A.s3_rstd[:, kc:kc + 1], A.s3_std[:, kc:kc + 1]), "sV", 1,
                      mark=f"v.rcp3.{g}.{k}")
                C.wait(f"v.rcp3.{g}.{k}")
                C.inc(e.tensor_mul(A.s3_bias[:, kc:kc + 1], A.s3_mneg[:, kc:kc + 1], A.s3_rstd[:, kc:kc + 1]),
                      "sV", 1, mark=f"v.rstd3.{g}.{k}")
            elif C.kind == "scalar":
                C.wait(f"v.mu3.{g}.{k}")
                if gk >= 2:
                    C.wait(f"v.tbf.{g if k >= 2 else g - 1}.{(k - 2) % 4}")
                C.inc(e.activation(A.xh3a[kc][:, :], A.out1[kc][:, :], AF.Square,
                                   bias=A.s3_mneg[:, kc:kc + 1], accum_out=A.s3_ssq[:, kc:kc + 1]),
                      "sS", 1, mark=f"s.sq3.{g}.{k}")
                C.wait(f"v.var3.{g}.{k}")
                C.inc(e.sqrt(A.s3_std[:, kc:kc + 1], A.s3_var[:, kc:kc + 1]), "sS", 1, mark=f"s.sqrt3.{g}.{k}")
                C.wait(f"v.rstd3.{g}.{k}")
                C.inc(e.activation(A.xh3a[kc][:, :], A.out1[kc][:, :], AF.Identity,
                                   bias=A.s3_bias[:, kc:kc + 1], scale=A.s3_rstd[:, kc:kc + 1]),
                      "sS", 1, mark=f"s.xhat3a.{g}.{k}")
            if C.kind == "vector":
                C.wait(f"s.xhat3a.{g}.{k}")
                C.wait(f"v.bcev.G1.{g}")
                ins = e.tensor_mul(A.tbf[kc][:, :], A.xh3a[kc][:, :], A.G1bc[:, :])
                C.inc(ins, "sV", 1, mark=f"v.tbf.{g}.{k}")
                C.wait(f"p.u.{g}.{k}")
                C.wait(f"v.tbf.{g}.{k}")
                if gk >= 2:
                    C.wait(f"g.x1sp.{g if k >= 2 else g - 1}.{(k - 2) % 4}")
                ins = e.tensor_tensor(A.x1t[kc][:, :], A.u_t[kc][:, :], A.tbf[kc][:, :], ALU.add)
                C.inc(ins, "sV", 1, mark=f"v.x1.{g}.{k}")
                C.wait(f"v.x1.{g}.{k}")
                C.inc(e.reduce_sum(A.s3_sum2[:, kc:kc + 1], A.x1t[kc][:, :], axis=AXX), "sV", 1,
                      mark=f"v.rs3b.{g}.{k}")
                C.wait(f"v.rs3b.{g}.{k}")
                C.inc(e.tensor_scalar_mul(A.s3_mneg2[:, kc:kc + 1], A.s3_sum2[:, kc:kc + 1], -1.0 / D),
                      "sV", 1, mark=f"v.mu3b.{g}.{k}")
                C.wait(f"s.sq3b.{g}.{k}")
                C.inc(e.tensor_scalar(A.s3_var2[:, kc:kc + 1], A.s3_ssq2[:, kc:kc + 1],
                                      scalar1=1.0 / D, scalar2=EPS, op0=ALU.mult, op1=ALU.add),
                      "sV", 1, mark=f"v.var3b.{g}.{k}")
                C.wait(f"s.sqrt3b.{g}.{k}")
                C.inc(e.reciprocal(A.s3_rstd2[:, kc:kc + 1], A.s3_std2[:, kc:kc + 1]), "sV", 1,
                      mark=f"v.rcp3b.{g}.{k}")
                C.wait(f"v.rcp3b.{g}.{k}")
                C.inc(e.tensor_mul(A.s3_bias2[:, kc:kc + 1], A.s3_mneg2[:, kc:kc + 1], A.s3_rstd2[:, kc:kc + 1]),
                      "sV", 1, mark=f"v.rstd3b.{g}.{k}")
            elif C.kind == "gpsimd":
                C.wait(f"v.x1.{g}.{k}")
                ins = e.dma_start(out=A.x1_hbm[g, k * 128:(k + 1) * 128, :], in_=A.x1t[kc][:, :])
                C.inc(ins, f"qGS{k % 2}", 16, mark=f"g.x1sp.{g}.{k}")
            if C.kind == "scalar":
                C.wait(f"v.mu3b.{g}.{k}")
                if gk >= 2:
                    C.wait(f"t.tr3.{g if k >= 2 else g - 1}.{(k - 2) % 4}.1")
                C.inc(e.activation(A.xh3b[kc][:, :], A.x1t[kc][:, :], AF.Square,
                                   bias=A.s3_mneg2[:, kc:kc + 1], accum_out=A.s3_ssq2[:, kc:kc + 1]),
                      "sS", 1, mark=f"s.sq3b.{g}.{k}")
                C.wait(f"v.var3b.{g}.{k}")
                C.inc(e.sqrt(A.s3_std2[:, kc:kc + 1], A.s3_var2[:, kc:kc + 1]), "sS", 1, mark=f"s.sqrt3b.{g}.{k}")
                C.wait(f"v.rstd3b.{g}.{k}")
                C.inc(e.activation(A.xh3b[kc][:, :], A.x1t[kc][:, :], AF.Identity,
                                   bias=A.s3_bias2[:, kc:kc + 1], scale=A.s3_rstd2[:, kc:kc + 1]),
                      "sS", 1, mark=f"s.xhat3b.{g}.{k}")
            if C.kind == "tensor":
                C.wait(f"s.xhat3b.{g}.{k}")
                for half in range(2):
                    th = gk * 2 + half
                    if th >= 1:
                        pg, pk, ph = (th - 1) // 8, ((th - 1) % 8) // 2, (th - 1) % 2
                        C.wait(f"v.h2ev.{pg}.{pk}.{ph}")
                    for fq in range(4):
                        fc = half * 4 + fq
                        ins = e.transpose(A.tr_ps2[:, fq * 128:(fq + 1) * 128],
                                          A.xh3b[kc][:, fc * 128:(fc + 1) * 128], A.ident_b[:, :])
                    C.inc(ins, "sT", 1, mark=f"t.tr3.{g}.{k}.{half}")
            if C.kind == "vector":
                for half in range(2):
                    C.wait(f"t.tr3.{g}.{k}.{half}")
                    if g >= 1 and k == 0 and half == 0:
                        C.wait(f"g.h2sp.{g - 1}")
                    for fq in range(4):
                        fc = half * 4 + fq
                        sbc = A.mod_fm[:, (24 + fc) * 8 + g:(24 + fc) * 8 + g + 1]
                        ins = e.tensor_scalar(A.h2cur[fc][:, k * 128:(k + 1) * 128],
                                              A.tr_ps2[:, fq * 128:(fq + 1) * 128],
                                              scalar1=A.onepcb[:, fc * 8 + g:fc * 8 + g + 1],
                                              scalar2=sbc, op0=ALU.mult, op1=ALU.add)
                        C.inc(ins, "sV", 1, mark=f"v.h2ev.{g}.{k}.{half}" if fq == 3 else None)
        if C.kind == "gpsimd":
            C.wait(f"v.h2ev.{g}.3.1")
            for fc in range(8):
                ins = e.dma_start(out=A.h2_hbm[g, fc], in_=A.h2cur[fc][:, :])
                C.inc(ins, "qGH", 16, mark=f"g.h2sp.{g}" if fc == 7 else None)


def emit_p4a(C, A):
    e = C.eng
    for nm in ["wf_T", "wi_T", "wc_T", "uf_T", "ui_T", "uc_T"]:
        emit_weight_dmas(C, A, nm)
    for nm, dst in [("wf_T", A.wgb[0]), ("wi_T", A.wgb[1]), ("wc_T", A.wgb[2]),
                    ("uf_T", A.ugb[0]), ("ui_T", A.ugb[1]), ("uc_T", A.ugb[2])]:
        emit_weight_casts(C, A, nm, dst)
    if C.kind == "vector":
        ins = e.memset(A.ones512[:, :], 1.0)
        C.inc(ins, "sV", 1, mark="v.ones512")
        C.wait("q.vecs")
        C.wait("v.ones512")
        for mch in range(8):
            ins = e.tensor_scalar_mul(A.c_f32[0][mch][:, :], A.ones512[:, :],
                                      A.vecs_fm[:, 48 + mch:48 + mch + 1])
            C.inc(ins, "sV", 1, mark=f"v.cadd.-1.{mch}")
    if C.kind == "scalar":
        for mch in range(8):
            C.wait(f"v.cadd.-1.{mch}")
            ins = e.activation(A.c_bf[0][mch][:, :], A.c_f32[0][mch][:, :], AF.Identity)
            C.inc(ins, "sS", 1, mark=f"s.cup.-1.{mch}")
    for t in range(NB):
        if C.kind == "sync":
            if t >= 2:
                C.wait(f"t.wdone.{t - 2}")
            C.wait(f"g.h2sp.{t}")
            for fc in range(8):
                ins = e.dma_start(out=A.h2stage[t % 2][fc][:, :], in_=A.h2_hbm[t, fc])
                C.inc(ins, f"qH{t % 2}", 16, mark=f"q.h2a.{t}" if fc == 7 else None)
        for mch in range(8):
            if C.kind == "tensor":
                if mch == 0:
                    C.wait(f"q.h2a.{t}")
                    C.wait(f"s.cup.{t - 1}.7")
                    if t == 0:
                        for gi in range(3):
                            C.wait(_w_cast_mark(WOFF[["wf_T", "wi_T", "wc_T"][gi]] + 7))
                        for gi in range(3):
                            C.wait(_w_cast_mark(WOFF[["uf_T", "ui_T", "uc_T"][gi]] + 7))
                for gate in range(3):
                    tm = t * 8 + mch
                    if tm >= 2:
                        pt, pm_ = (tm - 2) // 8, (tm - 2) % 8
                        C.wait(f"s.gact.{pt}.{pm_}.{gate}")
                    for ich in range(8):
                        e.matmul(A.pg[gate][mch % 2][:, :],
                                 A.wgb[gate][ich][:, mch * 128:(mch + 1) * 128],
                                 A.h2stage[t % 2][ich][:, :], start=(ich == 0), stop=False)
                    for m2 in range(8):
                        ins = e.matmul(A.pg[gate][mch % 2][:, :],
                                       A.ugb[gate][m2][:, mch * 128:(mch + 1) * 128],
                                       A.c_bf[t % 2][m2][:, :], start=False, stop=(m2 == 7))
                    C.inc(ins, "sT", 1, mark=f"t.g3.{t}.{mch}.{gate}")
                if mch == 7:
                    C.inc(_MockInstr(), "sT", 0, mark=f"t.wdone.{t}")
            elif C.kind == "scalar":
                tm = t * 8 + mch
                if tm >= 2:
                    pt, pm_ = (tm - 2) // 8, (tm - 2) % 8
                    C.wait(f"v.cadd.{pt}.{pm_}")
                for gate, (func, vb) in enumerate([(AF.Sigmoid, 2), (AF.Sigmoid, 3), (AF.Tanh, 4)]):
                    C.wait(f"t.g3.{t}.{mch}.{gate}")
                    ins = e.activation(A.gate_s[mch % 2][gate][:, :], A.pg[gate][mch % 2][:, :], func,
                                       bias=A.vecs_fm[:, vb * 8 + mch:vb * 8 + mch + 1])
                    C.inc(ins, "sS", 1, mark=f"s.gact.{t}.{mch}.{gate}")
            elif C.kind == "vector":
                C.wait(f"s.gact.{t}.{mch}.2")
                gs = A.gate_s[mch % 2]
                C.inc(e.tensor_mul(A.cw[:, :], gs[1][:, :], gs[2][:, :]), "sV", 1, mark=f"v.cw.{t}.{mch}")
                C.inc(e.tensor_mul(A.cw2[:, :], gs[0][:, :], A.c_f32[t % 2][mch][:, :]), "sV", 1,
                      mark=f"v.cw2.{t}.{mch}")
                C.wait(f"v.cw.{t}.{mch}")
                C.wait(f"v.cw2.{t}.{mch}")
                C.inc(e.tensor_tensor(A.c_f32[(t + 1) % 2][mch][:, :], A.cw2[:, :],
                                      A.cw[:, :], ALU.add), "sV", 1, mark=f"v.cadd.{t}.{mch}")
        if C.kind == "scalar":
            for mch in range(8):
                C.wait(f"v.cadd.{t}.{mch}")
                ins = e.activation(A.c_bf[(t + 1) % 2][mch][:, :], A.c_f32[(t + 1) % 2][mch][:, :], AF.Identity)
                C.inc(ins, "sS", 1, mark=f"s.cup.{t}.{mch}")


def emit_p4b(C, A):
    e = C.eng
    cfin = NB % 2
    emit_weight_dmas(C, A, "wo_T")
    emit_weight_dmas(C, A, "uo_T")
    emit_weight_casts(C, A, "wo_T", A.wob)
    emit_weight_casts(C, A, "uo_T", A.uob)
    emit_row_bcast(C, A, "BO", A.BObfm,
                   AP(A.BObfm[:, :].tensor, 0, [[8, 128], [1, 8]]), A.BObc,
                   war_mark=f"v.bcev.B1.{NB - 1}")
    for k in range(4):
        if C.kind == "tensor":
            if k == 0:
                C.wait(_w_cast_mark(WOFF["uo_T"] + 7))
                C.wait(f"s.cup.{NB - 1}.7")
            if k >= 1:
                C.wait(f"v.uoev.{k - 1}")
            for h in range(2):
                for m2 in range(8):
                    ins = e.matmul(A.o_ps[h][:, :],
                                   A.c_bf[cfin][m2][:, k * 128:(k + 1) * 128],
                                   A.uob[m2][:, h * 512:(h + 1) * 512], start=(m2 == 0), stop=(m2 == 7))
            C.inc(ins, "sT", 1, mark=f"t.uo.{k}")
        elif C.kind == "vector":
            C.wait(f"t.uo.{k}")
            C.wait("v.bcev.BO")
            for h in range(2):
                ins = e.tensor_tensor(A.uoterm[k][:, h * 512:(h + 1) * 512], A.o_ps[h][:, :],
                                      A.BObc[:, h * 512:(h + 1) * 512], ALU.add)
                C.inc(ins, "sV", 1, mark=f"v.uoev.{k}" if h == 1 else None)
    for g in range(NB):
        emit_row_bcast(C, A, f"GB.{g}", A.GBbfm,
                       AP(A.GBbfm[:, :].tensor, g, [[64, 128], [8, 8]]), A.GBbc,
                       war_mark=("v.bcev.BO" if g == 0 else f"v.tb2done.{g - 1}"))
        if C.kind == "sync":
            if g >= 2:
                C.wait(f"t.wo.{g - 2}.3")
            for fc in range(8):
                ins = e.dma_start(out=A.h2stage[g % 2][fc][:, :], in_=A.h2_hbm[g, fc])
                C.inc(ins, f"qH{g % 2}", 16, mark=f"q.h2b.{g}" if fc == 7 else None)
        for k in range(4):
            kc = k % 2
            gk = g * 4 + k
            if C.kind == "tensor":
                if k == 0:
                    C.wait(f"q.h2b.{g}")
                    if g == 0:
                        C.wait(_w_cast_mark(WOFF["wo_T"] + 7))
                        C.wait("v.uoev.3")
                if gk >= 1:
                    pgg, pk = (gk - 1) // 4, (gk - 1) % 4
                    C.wait(f"v.pre.{pgg}.{pk}")
                for h in range(2):
                    for ich in range(8):
                        ins = e.matmul(A.o_ps[h][:, :],
                                       A.h2stage[g % 2][ich][:, k * 128:(k + 1) * 128],
                                       A.wob[ich][:, h * 512:(h + 1) * 512], start=(ich == 0), stop=(ich == 7))
                C.inc(ins, "sT", 1, mark=f"t.wo.{g}.{k}")
            elif C.kind == "vector":
                C.wait(f"t.wo.{g}.{k}")
                if gk == 0:
                    C.wait("v.uoev.3")
                if gk >= 2:
                    C.wait(f"s.hbf.{g if k >= 2 else g - 1}.{(k - 2) % 4}")
                for h in range(2):
                    ins = e.tensor_tensor(A.pre_t[kc][:, h * 512:(h + 1) * 512], A.o_ps[h][:, :],
                                          A.uoterm[k][:, h * 512:(h + 1) * 512], ALU.add)
                    C.inc(ins, "sV", 1, mark=f"v.pre.{g}.{k}" if h == 1 else None)
                C.wait(f"s.hbf.{g}.{k}")
                C.wait(f"v.bcev.GB.{g}")
                if gk >= 2:
                    C.wait(f"g.oadd.{g if k >= 2 else g - 1}.{(k - 2) % 4}")
                ins = e.tensor_mul(A.tbf2[kc][:, :], A.hbf[kc][:, :], A.GBbc[:, :])
                C.inc(ins, "sV", 1, mark=f"v.tb2.{g}.{k}")
                if k == 3:
                    C.inc(_MockInstr(), "sV", 0, mark=f"v.tb2done.{g}")
            elif C.kind == "scalar":
                C.wait(f"v.pre.{g}.{k}")
                if gk >= 2:
                    C.wait(f"v.tb2.{g if k >= 2 else g - 1}.{(k - 2) % 4}")
                ins = e.activation(A.hbf[kc][:, :], A.pre_t[kc][:, :], AF.Tanh)
                C.inc(ins, "sS", 1, mark=f"s.hbf.{g}.{k}")
            elif C.kind == "sync":
                C.wait(f"g.x1sp.{g}.{k}")
                if gk >= 2:
                    C.wait(f"g.oadd.{g if k >= 2 else g - 1}.{(k - 2) % 4}")
                ins = e.dma_start(out=A.x1stage[kc][:, :], in_=A.x1_hbm[g, k * 128:(k + 1) * 128, :])
                C.inc(ins, f"qXR{k % 2}", 16, mark=f"q.x1r.{g}.{k}")
            elif C.kind == "gpsimd":
                C.wait(f"v.tb2.{g}.{k}")
                C.wait(f"q.x1r.{g}.{k}")
                ins = e.tensor_tensor(A.outt[kc][:, :], A.x1stage[kc][:, :], A.tbf2[kc][:, :], ALU.add)
                C.inc(ins, "qP", 1, mark=f"g.oadd.{g}.{k}")
                C.wait(f"g.oadd.{g}.{k}")
                ins = e.dma_start(out=A.dram["out"][g, k * 128:(k + 1) * 128, :], in_=A.outt[kc][:, :])
                C.inc(ins, "qGO", 16, mark=f"g.out.{g}.{k}")



def emit_fin(C, A):
    if C.kind == "gpsimd":
        for sem in ["qGO", "qGS0", "qGS1", "qGH", "qP"]:
            C.inc(_MockInstr(), sem, 0, mark=f"{sem}.end")
            C.wait(f"{sem}.end")


def build_program():
    nc = bass.Bass("TRN2", target_bir_lowering=False)
    A = SimpleNamespace()
    A.dram = {}
    A.dram["x_main"] = nc.dram_tensor("x_main", [NB, SLICE, D], F32, kind="ExternalInput").ap()
    A.dram["x_pref"] = nc.dram_tensor("x_pref", [NB, W, D], F32, kind="ExternalInput").ap()
    A.dram["pmask"] = nc.dram_tensor("pmask", [128, W], BF16, kind="ExternalInput").ap()
    A.dram["ident"] = nc.dram_tensor("ident", [128, 128], BF16, kind="ExternalInput").ap()
    A.dram["vecs"] = nc.dram_tensor("vecs", [128, 56], F32, kind="ExternalInput").ap()
    A.dram["c_in"] = nc.dram_tensor("c_in", [128, 64], F32, kind="ExternalInput").ap()
    A.dram["wmod"] = nc.dram_tensor("wmod", [D, 6 * D], F32, kind="ExternalInput").ap()
    for nm in WSEQ:
        A.dram[nm] = nc.dram_tensor(nm, [D, D], F32, kind="ExternalInput").ap()
    A.dram["out"] = nc.dram_tensor("out", [NB, SLICE, D], F32, kind="ExternalOutput").ap()
    scratch_kind = lambda n: "ExternalOutput" if n in TAPS else "Internal"
    A.x1_hbm = nc.dram_tensor("x1_hbm", [NB, SLICE, D], F32, kind=scratch_kind("x1")).ap()
    A.h2_hbm = nc.dram_tensor("h2_hbm", [NB, DCH, 128, SLICE], BF16, kind=scratch_kind("h2")).ap()
    if "mod" in TAPS:
        A.dram["tap_mod"] = nc.dram_tensor("tap_mod", [128, 384], F32, kind="ExternalOutput").ap()
    if "xb" in TAPS:
        A.dram["tap_xb"] = [nc.dram_tensor(f"tap_xb{i}", [128, NB * XBW], BF16, kind="ExternalOutput").ap()
                            for i in range(8)]
        A.dram["tap_xb2"] = [nc.dram_tensor(f"tap_xb2_{i}", [128, NB * XBW], BF16, kind="ExternalOutput").ap()
                             for i in range(8)]
    if "h1" in TAPS:
        A.dram["tap_h1"] = [nc.dram_tensor(f"tap_h1{i}", [128, XBW], BF16, kind="ExternalOutput").ap()
                            for i in range(8)]
    if "H" in TAPS:
        A.dram["tap_H"] = [nc.dram_tensor(f"tap_H{i}", [128, NB * SLICE], BF16, kind="ExternalOutput").ap()
                           for i in range(8)]

    C = Ctx()

    def run_phase(block, emit, A):
        for kind in KINDS:
            C.setup(kind, _MockEng(), True)
            emit(C, A)
        for kind in KINDS:
            def body(eng, kind=kind):
                C.setup(kind, eng, False)
                emit(C, A)
            getattr(block, kind)(body)

    with ExitStack() as st:
        sb = lambda n, s, d: st.enter_context(nc.sbuf_tensor(n, s, d))
        A.ident_b = sb("ident_b", [128, 128], BF16)
        A.vecs_fm = sb("vecs_fm", [128, 56], F32)
        A.c_fm = sb("c_fm", [128, 64], F32)
        A.silu_c = sb("silu_c", [128, 64], F32)
        A.silu_cb = sb("silu_cb", [128, 64], BF16)
        A.mod_fm = sb("mod_fm", [128, 384], F32)
        A.onepca = sb("onepca", [128, 64], F32)
        A.onepcb = sb("onepcb", [128, 64], F32)
        A.G1bfm = sb("G1bfm", [128, 64], BF16)
        A.B1bfm = sb("B1bfm", [128, 64], BF16)
        A.GBbfm = sb("GBbfm", [128, 64], BF16)
        A.BObfm = sb("BObfm", [128, 8], BF16)
        A.ones_b = sb("ones_b", [1, 128], BF16)
        A.pmask_b = sb("pmask_b", [128, W], BF16)
        A.wstage = [sb(f"wstage{i}", [128, D], F32) for i in range(2)]
        A.rowstage = sb("rowstage", [8, 128], BF16)
        A.rowflat = sb("rowflat", [1, D], BF16)
        for s in ["qC0", "qC1", "qC2", "qC3", "qM0", "qM1", "qW0", "qW1", "qX0", "qX1",
                  "qR", "qX3_0", "qX3_1", "qH0", "qH1", "qXR0", "qXR1",
                  "qGS0", "qGS1", "qGH", "qGO", "qP", "sT", "sV", "sS"]:
            C.sems[s] = st.enter_context(nc.semaphore(s))
            C.counts[s] = 0
        block = st.enter_context(nc.Block())

        with ExitStack() as s0:
            sb0 = lambda n, s, d: s0.enter_context(nc.sbuf_tensor(n, s, d))
            A.wmodstage = [sb0(f"wmodstage{i}", [128, 6 * D], F32) for i in range(2)]
            A.wmod_b = [sb0(f"wmod_b{i}", [128, 6 * D], BF16) for i in range(2)]
            A.ga_cl = sb0("ga_cl", [128, 64], F32)
            A.gb_cl = sb0("gb_cl", [128, 64], F32)
            A.cl_m1 = sb0("cl_m1", [128, 64], F32)
            A.cl_m2 = sb0("cl_m2", [128, 64], F32)
            A.cl_mask = sb0("cl_mask", [128, 64], U8)
            A.G1_fm = sb0("G1_fm", [128, 64], F32)
            A.B1_fm = sb0("B1_fm", [128, 64], F32)
            A.mod_ps = [s0.enter_context(nc.psum_tensor(f"mod_ps{i}", [128, 384], F32)) for i in range(2)]
            if "p0" in PHASES:
                run_phase(block, emit_p0, A)

        with ExitStack() as sH:
            A.H_valid = [sH.enter_context(nc.sbuf_tensor(f"Hv{i}", [128, NB * SLICE], BF16)) for i in range(8)]
            with ExitStack() as sC:
                A.C_Tb = [sC.enter_context(nc.sbuf_tensor(f"CT{i}", [128, D], BF16)) for i in range(8)]
                with ExitStack() as sXB:
                    A.xb_b = [sXB.enter_context(nc.sbuf_tensor(f"xb{i}", [128, NB * XBW], BF16)) for i in range(8)]
                    with ExitStack() as s1:
                        sb1 = lambda n, s, d: s1.enter_context(nc.sbuf_tensor(n, s, d))
                        A.B_Tb = [sb1(f"BT{i}", [128, D], BF16) for i in range(8)]
                        A.h1_fm = [sb1(f"h1_{i}", [128, XBW], BF16) for i in range(8)]
                        A.xstage = [sb1(f"xst{i}", [128, D], F32) for i in range(2)]
                        A.xh_bf = [sb1(f"xh{i}", [128, D], BF16) for i in range(2)]
                        A.s_sum = sb1("s_sum", [128, 2], F32)
                        A.s_mneg = sb1("s_mneg", [128, 2], F32)
                        A.s_ssq = sb1("s_ssq", [128, 2], F32)
                        A.s_var = sb1("s_var", [128, 2], F32)
                        A.s_std = sb1("s_std", [128, 2], F32)
                        A.s_rstd = sb1("s_rstd", [128, 2], F32)
                        A.s_bias = sb1("s_bias", [128, 2], F32)
                        A.h1mm = A.H_valid
                        A.h1stage = sb1("h1stage", [128, 512], BF16)
                        A.h1junk = sb1("h1junk", [128, 128], BF16)
                        A.tr_ps = [s1.enter_context(nc.psum_tensor(f"trp{i}", [128, 512], BF16)) for i in range(2)]
                        A.dl_ps = s1.enter_context(nc.psum_tensor("dl_ps", [128, 128], F32))
                        A.xb_ps = [s1.enter_context(nc.psum_tensor(f"xbp{i}", [128, 512], F32)) for i in range(4)]
                        if "p1" in PHASES:
                            run_phase(block, emit_p1, A)
                    with ExitStack() as s2:
                        sb2 = lambda n, s, d: s2.enter_context(nc.sbuf_tensor(n, s, d))
                        A.A_Tb = [sb2(f"AT{i}", [128, D], BF16) for i in range(8)]
                        A.h_par = [[sb2(f"hp{p}_{i}", [128, 128], BF16) for i in range(8)] for p in range(2)]
                        A.rec_tmp = [[sb2(f"rt{p}_{i}", [128, 128], F32) for i in range(8)] for p in range(2)]
                        A.pm = [s2.enter_context(nc.psum_tensor(f"pm{i}", [128, 128], F32)) for i in range(8)]
                        if "p2" in PHASES:
                            run_phase(block, emit_p2, A)
                with ExitStack() as s3:
                    sb3 = lambda n, s, d: s3.enter_context(nc.sbuf_tensor(n, s, d))
                    A.x3stage = [sb3(f"x3_{i}", [128, D], F32) for i in range(2)]
                    A.out1 = [sb3(f"o1_{i}", [128, D], F32) for i in range(2)]
                    A.u_t = [sb3(f"u_{i}", [128, D], F32) for i in range(2)]
                    A.x1t = [sb3(f"x1_{i}", [128, D], F32) for i in range(2)]
                    A.xh3a = [sb3(f"xh3a{i}", [128, D], BF16) for i in range(2)]
                    A.xh3b = [sb3(f"xh3b{i}", [128, D], BF16) for i in range(2)]
                    A.tbf = [sb3(f"tbf{i}", [128, D], BF16) for i in range(2)]
                    A.G1bc = sb3("G1bc", [128, D], BF16)
                    A.B1bc = sb3("B1bc", [128, D], BF16)
                    A.h2cur = [sb3(f"h2c{i}", [128, SLICE], BF16) for i in range(8)]
                    for nm in ["s3_sum", "s3_mneg", "s3_ssq", "s3_var", "s3_std", "s3_rstd", "s3_bias",
                               "s3_sum2", "s3_mneg2", "s3_ssq2", "s3_var2", "s3_std2", "s3_rstd2", "s3_bias2"]:
                        setattr(A, nm, sb3(nm, [128, 2], F32))
                    A.y_ps2 = [[s3.enter_context(nc.psum_tensor(f"yp{i}_{h}", [128, 512], F32)) for h in range(2)]
                               for i in range(2)]
                    A.tr_ps2 = s3.enter_context(nc.psum_tensor("trq", [128, 512], BF16))
                    A.rowT_ps = s3.enter_context(nc.psum_tensor("rowT_ps", [8, 128], BF16))
                    A.bc_ps = s3.enter_context(nc.psum_tensor("bcp", [128, 1024], F32))
                    if "p3" in PHASES:
                        run_phase(block, emit_p3, A)

        with ExitStack() as s4:
            sb4 = lambda n, s, d: s4.enter_context(nc.sbuf_tensor(n, s, d))
            A.h2stage = [[sb4(f"h2s{p}_{i}", [128, SLICE], BF16) for i in range(8)] for p in range(2)]
            A.c_f32 = [[sb4(f"cf{p}_{i}", [128, SLICE], F32) for i in range(8)] for p in range(2)]
            A.c_bf = [[sb4(f"cb{p}_{i}", [128, SLICE], BF16) for i in range(8)] for p in range(2)]
            with ExitStack() as s4a:
                sba = lambda n, s, d: s4a.enter_context(nc.sbuf_tensor(n, s, d))
                A.wgb = [[sba(f"wg{gi}_{i}", [128, D], BF16) for i in range(8)] for gi in range(3)]
                A.ugb = [[sba(f"ug{gi}_{i}", [128, D], BF16) for i in range(8)] for gi in range(3)]
                A.gate_s = [[sba(f"gs{p}_{gi}", [128, SLICE], F32) for gi in range(3)] for p in range(2)]
                A.cw = sba("cw", [128, SLICE], F32)
                A.cw2 = sba("cw2", [128, SLICE], F32)
                A.ones512 = sba("ones512", [128, SLICE], F32)
                A.pg = [[s4a.enter_context(nc.psum_tensor(f"pg{gi}_{p}", [128, 512], F32)) for p in range(2)]
                        for gi in range(3)]
                if "p4a" in PHASES:
                    run_phase(block, emit_p4a, A)
            with ExitStack() as s4b:
                sbb = lambda n, s, d: s4b.enter_context(nc.sbuf_tensor(n, s, d))
                A.wob = [sbb(f"wob{i}", [128, D], BF16) for i in range(8)]
                A.uob = [sbb(f"uob{i}", [128, D], BF16) for i in range(8)]
                A.uoterm = [sbb(f"uot{i}", [128, D], F32) for i in range(4)]
                A.GBbc = sbb("GBbc", [128, D], BF16)
                A.BObc = sbb("BObc", [128, D], BF16)
                A.x1stage = [sbb(f"x1s{i}", [128, D], F32) for i in range(2)]
                A.pre_t = [sbb(f"pre{i}", [128, D], F32) for i in range(2)]
                A.hbf = [sbb(f"hbf{i}", [128, D], BF16) for i in range(2)]
                A.tbf2 = [sbb(f"tbf2{i}", [128, D], BF16) for i in range(2)]
                A.outt = [sbb(f"outt{i}", [128, D], F32) for i in range(2)]
                A.o_ps = [s4b.enter_context(nc.psum_tensor(f"op{h}", [128, 512], F32)) for h in range(2)]
                A.rowT_ps = s4b.enter_context(nc.psum_tensor("rowT_ps2", [8, 128], BF16))
                A.bc_ps = s4b.enter_context(nc.psum_tensor("bcq", [128, 1024], F32))
                if "p4b" in PHASES:
                    run_phase(block, emit_p4b, A)

        run_phase(block, emit_fin, A)
        assert C.counts == C.counts_real, f"count drift: {C.counts} vs {C.counts_real}"

    return nc


def _pack_vecs(vals):
    cols = []
    for v in vals:
        cols.append(np.ascontiguousarray(v.reshape(8, 128).T))
    return np.ascontiguousarray(np.concatenate(cols, axis=1), dtype=np.float32)


def kernel(L0_attn_x, L0_attn_A, L0_attn_B, L0_attn_C, L0_attn_gamma, L0_attn_beta,
           L0_ffn_wf, L0_ffn_uf, L0_ffn_bf, L0_ffn_wi, L0_ffn_ui, L0_ffn_bi,
           L0_ffn_wc, L0_ffn_uc, L0_ffn_bc, L0_ffn_wo, L0_ffn_uo, L0_ffn_bo,
           L0_ffn_in_c, L0_c, L0_wmod):
    x = np.ascontiguousarray(np.asarray(L0_attn_x, dtype=np.float32))
    nc = build_program()

    tr = lambda a: np.ascontiguousarray(np.asarray(a, dtype=np.float32).T)
    shared = {
        "ident": np.eye(128, dtype=ml_dtypes.bfloat16),
        "vecs": _pack_vecs([np.asarray(v, np.float32) for v in
                            [L0_attn_gamma, L0_attn_beta, L0_ffn_bf, L0_ffn_bi, L0_ffn_bc, L0_ffn_bo,
                             L0_ffn_in_c]]),
        "c_in": np.ascontiguousarray(
            np.asarray(L0_c, np.float32).T.reshape(8, 128, 8).transpose(1, 0, 2).reshape(128, 64)),
        "wmod": np.ascontiguousarray(np.asarray(L0_wmod, np.float32)),
        "A_T": tr(L0_attn_A), "B_T": tr(L0_attn_B), "C_T": tr(L0_attn_C),
        "wf_T": tr(L0_ffn_wf), "wi_T": tr(L0_ffn_wi), "wc_T": tr(L0_ffn_wc),
        "uf_T": tr(L0_ffn_uf), "ui_T": tr(L0_ffn_ui), "uc_T": tr(L0_ffn_uc),
        "wo_T": tr(L0_ffn_wo), "uo_T": tr(L0_ffn_uo),
    }
    in_maps = []
    for c in range(8):
        m = dict(shared)
        m["x_main"] = np.ascontiguousarray(x[:, c * SLICE:(c + 1) * SLICE, :])
        if c == 0:
            m["x_pref"] = np.zeros((NB, W, D), np.float32)
            m["pmask"] = np.zeros((128, W), dtype=ml_dtypes.bfloat16)
        else:
            m["x_pref"] = np.ascontiguousarray(x[:, c * SLICE - W:c * SLICE, :])
            m["pmask"] = np.ones((128, W), dtype=ml_dtypes.bfloat16)
        in_maps.append(m)

    res = run_bass_kernel_spmd(nc, in_maps, core_ids=list(range(8)))
    out = np.concatenate([r["out"] for r in res.results], axis=1)
    if TAPS:
        kernel.last_results = res
    return out
